# version 15
# baseline (speedup 1.0000x reference)
import os
import sys
sys.path.insert(0, "/opt/trn_rl_repo")
import numpy as np
import ml_dtypes
import concourse.bass as bass
import concourse.mybir as mybir
from concourse.bass_utils import run_bass_kernel_spmd

F32 = mybir.dt.float32
BF16 = mybir.dt.bfloat16
AF = mybir.ActivationFunctionType
ALU = mybir.AluOpType
AX = mybir.AxisListType

ENGS = ("tensor", "vector", "scalar", "gpsimd", "sync")
SHUF_MASK = []
for _i in range(16):
    SHUF_MASK += [2 * _i + 1, 2 * _i]


class Cfg:
    def __init__(self, tiny=False):
        if tiny:
            self.B, self.T, self.H = 1, 1024, 512
            self.NH, self.NKV = 8, 2
            self.E, self.KE, self.IE = 128, 8, 512
            self.ISH = 256
        else:
            self.B, self.T, self.H = 4, 4096, 2048
            self.NH, self.NKV = 32, 8
            self.E, self.KE, self.IE = 128, 8, 1024
            self.ISH = 1024
        self.DH = 64
        self.HC = self.H // 128
        self.SLOT = 256
        self.NSLOT = self.T // (2 * self.SLOT)
        self.NQ = self.NSLOT // 2
        self.NCORE = 2 * self.B
        self.KV = self.T
        self.NKT = self.T // 128
        self.OWN = self.NSLOT * self.SLOT
        self.KC = self.NKV * self.DH // 128
        self.GCOL = self.KE * self.IE
        self.SHC = 2 * self.ISH
        self.CS = 256
        self.OS = 256
        self.QEPS = 1e-8
        self.LNEPS = 1e-5
        self.EXPB = -30.0
        self.NEGB = -330.0

    def trips(self, j):
        return 4 * j + 4

    def bias_col(self, j, sc):
        return sum(self.trips(jj) for jj in range(j)) + sc

    @property
    def nbias(self):
        return sum(self.trips(j) for j in range(self.NSLOT))


class Op:
    __slots__ = ("eng", "fn", "idx", "waits", "inc", "is_dma", "label")

    def __init__(self, eng, fn, idx, is_dma, label):
        self.eng, self.fn, self.idx = eng, fn, idx
        self.waits = []
        self.inc = None
        self.is_dma = is_dma
        self.label = label


class PB:
    def __init__(self, nc):
        self.nc = nc
        self.ops = {e: [] for e in ENGS}
        self.sems = {}
        self.eng_inc_count = {e: 0 for e in ENGS}
        self.eng_incs = {e: [] for e in ENGS}
        self.chan_total = {}
        self.last_writer = {}
        self.readers = {}
        self.clock = {e: {} for e in ENGS}
        self.n_waits = 0

    def sem(self, name):
        if name not in self.sems:
            self.sems[name] = self.nc.alloc_semaphore(name.replace(":", "_"))
        return self.sems[name]

    def _resolve_eng_dep(self, eng, idx):
        incs = self.eng_incs[eng]
        lo, hi = 0, len(incs)
        while lo < hi:
            mid = (lo + hi) // 2
            if incs[mid][0] >= idx:
                hi = mid
            else:
                lo = mid + 1
        if lo < len(incs):
            return ("E:" + eng, incs[lo][1])
        op = self.ops[eng][idx]
        assert op.inc is None
        self.eng_inc_count[eng] += 1
        val = self.eng_inc_count[eng]
        op.inc = ("E:" + eng, val)
        incs.append((idx, val))
        return ("E:" + eng, val)

    def _need(self, me_eng, waits, dep):
        deng, didx = dep
        op = self.ops[deng][didx]
        if op.is_dma:
            sem = op.inc[0]
            val = self.chan_total[sem]
        else:
            sem, val = self._resolve_eng_dep(deng, didx)
        myc = self.clock[me_eng]
        if myc.get(sem, 0) < val:
            waits[sem] = max(waits.get(sem, 0), val)
            myc[sem] = val

    def add(self, eng, fn, reads=(), writes=(), dma_chan=None, label=""):
        idx = len(self.ops[eng])
        is_dma = dma_chan is not None
        op = Op(eng, fn, idx, is_dma, label)
        self.ops[eng].append(op)
        waits = {}
        deps = []
        for k in reads:
            w = self.last_writer.get(k)
            if w is not None:
                deps.append(w)
        for k in writes:
            w = self.last_writer.get(k)
            if w is not None:
                deps.append(w)
            deps.extend(self.readers.get(k, ()))
        me = (eng, idx)
        for dep in deps:
            if dep == me:
                continue
            if eng == "tensor" and dep[0] == "tensor":
                continue
            self._need(eng, waits, dep)
        op.waits = sorted(waits.items())
        self.n_waits += len(op.waits)
        if is_dma:
            tot = self.chan_total.get(dma_chan, 0) + 16
            self.chan_total[dma_chan] = tot
            op.inc = (dma_chan, tot)
        for k in writes:
            self.last_writer[k] = me
            self.readers[k] = []
        for k in reads:
            self.readers.setdefault(k, []).append(me)
        return me

    def emit(self, block, final_eng="sync"):
        for e in ENGS:
            if self.eng_inc_count[e]:
                self.sem("E:" + e)
        for chan in sorted(self.chan_total):
            self.sem(chan)
        engines = {"tensor": block.tensor, "vector": block.vector,
                   "scalar": block.scalar, "gpsimd": block.gpsimd,
                   "sync": block.sync}

        def mk(eng_name):
            def section(eng):
                for op in self.ops[eng_name]:
                    for sname, val in op.waits:
                        eng.wait_ge(self.sems[sname], val)
                    inst = op.fn(eng)
                    if op.inc is not None:
                        inst.then_inc(self.sems[op.inc[0]],
                                      16 if op.is_dma else 1)
                if eng_name == final_eng:
                    for chan, tot in sorted(self.chan_total.items()):
                        eng.wait_ge(self.sems[chan], tot)
                    for e in ENGS:
                        if e != eng_name and self.eng_inc_count[e]:
                            eng.wait_ge(self.sems["E:" + e], self.eng_inc_count[e])
            return section

        for name, dec in engines.items():
            dec(mk(name))


def host_prep(cfg, inputs):
    f32 = np.float32
    H = cfg.H
    Wq = np.asarray(inputs["L0_attn_Wq"], f32)
    Wk = np.asarray(inputs["L0_attn_Wk"], f32)
    Wv = np.asarray(inputs["L0_attn_Wv"], f32)
    Wo = np.asarray(inputs["L0_attn_Wo"], f32)
    w_router = np.asarray(inputs["L0_ffn_w_router"], f32)
    w_gate = np.asarray(inputs["L0_ffn_w_gate"], f32)
    w_up = np.asarray(inputs["L0_ffn_w_up"], f32)
    w_down = np.asarray(inputs["L0_ffn_w_down"], f32)
    amps = np.asarray(inputs["L0_ffn_amplitudes"], f32)
    escale = np.asarray(inputs["L0_ffn_expert_scale"], f32)
    swg = np.asarray(inputs["L0_ffn_shared_w_gate"], f32)
    swu = np.asarray(inputs["L0_ffn_shared_w_up"], f32)
    swd = np.asarray(inputs["L0_ffn_shared_w_down"], f32)
    x = np.asarray(inputs["L0_attn_x"], f32)

    qperm = []
    for c in range(cfg.NKV // 2):
        for i in range(4):
            for h in (8 * c + i, 8 * c + 4 + i):
                qperm.extend(range(64 * h, 64 * h + 64))
    qperm = np.array(qperm)

    def slab(w, cs):
        K, C = w.shape
        return np.ascontiguousarray(
            w.reshape(K // 128, 128, C // cs, cs).transpose(2, 1, 0, 3))

    shared = {
        "wq": slab(np.ascontiguousarray(Wq.T)[:, qperm], 256),
        "wk": slab(np.ascontiguousarray(Wk.T), Wk.shape[0]),
        "wv": slab(np.ascontiguousarray(Wv.T), Wv.shape[0]),
        "wo": slab(np.ascontiguousarray(Wo.T)[qperm, :], cfg.OS),
        "wr": slab(w_router, cfg.E),
        "wg": slab(w_gate, cfg.CS),
        "wu": slab(w_up, cfg.CS),
        "wd": slab(w_down, cfg.OS),
        "swg": slab(swg, cfg.CS),
        "swu": slab(swu, cfg.CS),
        "swd": slab(swd, cfg.OS),
        "amps": np.ascontiguousarray(amps.reshape(cfg.E, 2 * cfg.KE)),
        "escale": np.ascontiguousarray(escale.reshape(cfg.E, 1)),
    }
    shared_ident = np.zeros((128, 256), f32)
    shared_ident[:, 0:128] = np.eye(128, dtype=f32)
    shared_ident[64, 128 + np.arange(64)] = 1.0
    shared_ident[63, 192 + np.arange(64)] = 1.0
    tri = np.zeros((128, 2, cfg.SLOT), f32)
    for dd in range(2):
        tri[:, dd, :] = (128 * dd + np.arange(128)[:, None]
                         <= np.arange(cfg.SLOT)[None, :])
    shared["tri"] = tri
    shared["ident"] = shared_ident

    invf = 1.0 / 10000.0 ** (np.arange(0, cfg.DH, 2, dtype=f32) / cfg.DH)
    didx = np.arange(cfg.DH)
    ang_i = didx % (cfg.DH // 2)
    sign = np.where(didx % 2 == 0, -1.0, 1.0).astype(f32)

    per_core = []
    for core in range(cfg.NCORE):
        seq, par = core // 2, core % 2
        t_own = [2 * j + par for j in range(cfg.NSLOT)]
        t_oth = [2 * j + (1 - par) for j in range(cfg.NSLOT)]
        xs = x[seq].reshape(2 * cfg.NSLOT, cfg.SLOT, H)
        x_own = np.ascontiguousarray(xs[t_own].reshape(cfg.OWN, H))
        x_oth = np.ascontiguousarray(xs[t_oth].reshape(cfg.OWN, H))
        pos = np.concatenate(
            [cfg.SLOT * t + np.arange(cfg.SLOT) for t in t_own] +
            [cfg.SLOT * t + np.arange(cfg.SLOT) for t in t_oth]).astype(f32)
        ang = pos[None, :] * invf[ang_i][:, None]
        ctab = np.ascontiguousarray(
            np.tile(np.cos(ang), (2, 1)).astype(ml_dtypes.bfloat16))
        stab = np.ascontiguousarray(
            np.tile(np.sin(ang) * sign[:, None],
                    (2, 1)).astype(ml_dtypes.bfloat16))
        biasm = np.full((128, cfg.nbias), cfg.EXPB, f32)
        for j in range(cfg.NSLOT):
            q0 = cfg.SLOT * t_own[j]
            for oc in range(2 * j + 2):
                col = cfg.bias_col(j, 2 * j + 2 + oc)
                kpos = (cfg.SLOT * t_oth[oc // 2] + 128 * (oc % 2)
                        + np.arange(128))
                biasm[:, col] = np.where(kpos < q0, cfg.EXPB, cfg.NEGB)
        per_core.append({"x_own": x_own, "x_oth": x_oth, "rcos": ctab,
                         "rsin": stab, "biasm": biasm})
    return shared, per_core


def build_nc(cfg):
    nc = bass.Bass("TRN2", target_bir_lowering=False)
    H, HC, KC = cfg.H, cfg.HC, cfg.KC
    SLOT, NSLOT, OWN, KV = cfg.SLOT, cfg.NSLOT, cfg.OWN, cfg.KV
    CS, OS, KE, IE, E = cfg.CS, cfg.OS, cfg.KE, cfg.IE, cfg.E
    KVD = cfg.NKV * 64
    NIC = IE // 128
    NSC = cfg.SHC // 128

    def din(n, s, k="ExternalInput"):
        return nc.dram_tensor(n, s, F32, kind=k).ap()

    x_own = din("x_own", [OWN, H])
    x_oth = din("x_oth", [OWN, H])
    rcos = nc.dram_tensor("rcos", [128, 2 * OWN], BF16, kind="ExternalInput").ap()
    rsin = nc.dram_tensor("rsin", [128, 2 * OWN], BF16, kind="ExternalInput").ap()
    biasm = din("biasm", [128, cfg.nbias])
    tri_in = din("tri", [128, 2, SLOT])
    amps_in = din("amps", [E, 2 * KE])
    id_in = din("ident", [128, 256])
    esc_in = din("escale", [E, 1])
    wq_in = din("wq", [H // 256, 128, HC, 256])
    wk_in = din("wk", [1, 128, HC, KVD])
    wv_in = din("wv", [1, 128, HC, KVD])
    wo_in = din("wo", [H // OS, 128, HC, OS])
    wr_in = din("wr", [1, 128, HC, E])
    wg_in = din("wg", [cfg.GCOL // CS, 128, HC, CS])
    wu_in = din("wu", [cfg.GCOL // CS, 128, HC, CS])
    wd_in = din("wd", [H // OS, 128, NIC, OS])
    swg_in = din("swg", [cfg.SHC // CS, 128, HC, CS])
    swu_in = din("swu", [cfg.SHC // CS, 128, HC, CS])
    swd_in = din("swd", [H // OS, 128, NSC, OS])
    out_d = din("out", [OWN, H], k="ExternalOutput")

    ctx = []

    def sb(name, shape, dtype, **kw):
        t = nc.sbuf_tensor(name, shape, dtype, **kw)
        ctx.append(t)
        return t.__enter__()

    def psum(name, shape):
        t = nc.psum_tensor(name, shape, F32)
        ctx.append(t)
        return t.__enter__()

    kT = sb("kT", [128, KC, KV], BF16)
    v_sb = sb("v_sb", [128, cfg.NKT, 516 * KVD // 512], BF16)
    cos_sb = sb("cos_sb", [128, 2 * OWN], BF16)
    sin_sb = sb("sin_sb", [128, 2 * OWN], BF16)
    bias_sb = sb("bias_sb", [128, cfg.nbias], F32)
    tri_sb = sb("tri_sb", [128, 2, SLOT], BF16)
    id_bf = sb("id_bf", [128, 256], BF16)
    amat = sb("amat", [128, KE], BF16)
    x_buf = sb("x_buf", [128, 2, H], F32)
    xslice = sb("xslice", [128, 2, OS], F32)
    xn_bf = sb("xn_bf", [128, H], BF16)
    actT = sb("actT", [128, HC, 512], BF16, is_transpose_dest=True)
    shT = sb("shT", [128, NSC, 512], BF16, is_transpose_dest=True)
    yT = sb("yT", [128, NIC, 512], BF16, is_transpose_dest=True)
    y_acc = sb("y_acc", [128, 4, 256], F32)
    sg_bf = sb("sg_bf", [128, 2, CS], BF16)
    su_bf = sb("su_bf", [128, 2, CS], BF16)
    bas_bf = sb("bas_bf", [128, 2, CS], BF16)
    sh_bf = bas_bf
    exp_bf = sb("exp_bf", [128, 3, 1024], BF16)
    g_bf = sb("g_bf", [128, 128], BF16)
    gT_bf = sb("gT_bf", [128, 128], BF16, is_transpose_dest=True)
    c_sb = sb("c_sb", [128, 4, KE], F32)
    zq = sb("zq", [128, 4, 256], BF16)
    ybf_s = sb("ybf_s", [128, 4, 256], BF16)
    scr = sb("scr", [128, 3, 256], F32)
    expd_bf = scr[:, 0:2, :].rearrange("p a b -> p (a b)").bitcast(BF16)
    scr2 = sb("scr2", [128, 64], F32)
    ln6 = scr2[:, 0:24]
    ost = sb("ost", [128, 2, OS], F32)
    arena = sb("arena", [128, 20480], BF16)

    def ar(off_kb, shape):
        n = int(np.prod(shape))
        off = off_kb * 512
        v = arena[:, off:off + n]
        if len(shape) == 2:
            v = v.rearrange("p (a b) -> p a b", a=shape[0], b=shape[1])
        size_kb = (2 * n + 1023) // 1024
        pages = tuple(("ar", i) for i in
                      range(off_kb // 4, (off_kb + size_kb + 3) // 4))
        return v, pages

    kb_wkv = (HC * KVD * 2 + 1023) // 1024
    wk_sb, wk_pg = ar(0, (HC, KVD))
    wv_sb, wv_pg = ar(16, (HC, KVD))
    wr_sb, wr_pg = ar(32, (HC, E))
    kb_gu = (HC * CS * 2 + 1023) // 1024
    wgu = []
    for w in range(2):
        wgu.append([ar(kb_gu * b + 2 * kb_gu * w, (HC, CS)) for b in range(2)])
    wq_b = [ar(kb_gu * b, (HC, 256)) for b in range(2)]
    qT_v, qT_pg = ar(24, (HC, 512))
    kb_o = (HC * OS * 2 + 1023) // 1024
    kb_d = (NIC * OS * 2 + 1023) // 1024
    kb_s = (NSC * OS * 2 + 1023) // 1024
    kb_b6 = kb_o + kb_d + kb_s
    b6b = []
    for b in range(2):
        base = kb_b6 * b
        o, opg = ar(base, (HC, OS))
        d_, dpg = ar(base + kb_o, (NIC, OS))
        s_, spg = ar(base + kb_o + kb_d, (NSC, OS))
        b6b.append(((o, opg), (d_, dpg), (s_, spg)))

    pbig = psum("pbig", [128, 1024])
    pbig2 = psum("pbig2", [128, 1024])
    P256 = [psum("p256_%d" % i, [128, 256]) for i in range(4)]

    p = PB(nc)

    def d(eng, dst, src, chan, reads=(), writes=(), label=""):
        return p.add(eng, lambda e, dst=dst, src=src: e.dma_start(dst, src),
                     reads=reads, writes=writes, dma_chan=chan, label=label)

    amp_sb = scr2[:, 24:24 + 2 * KE]
    esc_sb = scr2[:, 40:41]
    d("sync", cos_sb[:, :], rcos, "D:pre", writes=[("cos",)])
    d("sync", sin_sb[:, :], rsin, "D:pre", writes=[("sin",)])
    d("sync", bias_sb[:, :], biasm, "D:pre", writes=[("bias",)])
    d("gpsimd", tri_sb[:, :, :], tri_in, "D:preg", writes=[("tri",)])
    d("gpsimd", id_bf[:, :], id_in, "D:preg", writes=[("id",)])
    d("sync", amp_sb, amps_in, "D:pre", writes=[("amp",)])
    d("sync", esc_sb, esc_in, "D:pre", writes=[("esc",)])
    p.add("vector",
          lambda e: e.memset(v_sb[:, :, :].rearrange(
              "p s (a b) -> p s a b", a=cfg.NKV // 2)[:, :, :, 64:65], 1.0),
          writes=[("vones",)])
    p.add("vector", lambda e: e.memset(zq[:, :, :], 0.0),
          writes=[("zq", v) for v in range(4)])

    a0 = amp_sb.rearrange("p (k two) -> p k two", two=2)[:, :, 0]
    a1 = amp_sb.rearrange("p (k two) -> p k two", two=2)[:, :, 1]
    t0 = scr2[:, 0:KE]
    t1 = scr2[:, 8:16]
    K0, K1 = ("ln6", 0), ("ln6", 1)
    p.add("vector", lambda e: e.tensor_mul(t0, a0, a0),
          reads=[("amp",)], writes=[K0])
    p.add("vector", lambda e: e.tensor_mul(t1, a1, a1),
          reads=[("amp",)], writes=[K1])
    p.add("vector", lambda e: e.tensor_add(t0, t0, t1),
          reads=[K0, K1], writes=[K0])
    ssum = scr2[:, 17:18]
    rsum = scr2[:, 18:19]
    p.add("vector", lambda e: e.reduce_sum(ssum, t0, axis=AX.X),
          reads=[K0], writes=[("ssum",)])
    p.add("vector", lambda e: e.tensor_scalar_add(ssum, ssum, cfg.QEPS),
          reads=[("ssum",)], writes=[("ssum",)])
    p.add("vector", lambda e: e.reciprocal(rsum, ssum),
          reads=[("ssum",)], writes=[("rsum",)])
    p.add("vector", lambda e: e.tensor_scalar(t0, t0, rsum, None, ALU.mult),
          reads=[K0, ("rsum",)], writes=[K0])
    p.add("vector", lambda e: e.tensor_scalar(amat[:, :], t0, esc_sb, None,
                                              ALU.mult),
          reads=[K0, ("esc",)], writes=[("amat",)])

    def layer_norm(xk, xa, outk, out_ap):
        nst = H // 512
        for i in range(nst):
            p.add("vector",
                  lambda e, i=i: e.bn_stats(ln6[:, 6 * i:6 * i + 6],
                                            xa[:, 512 * i:512 * i + 512]),
                  reads=[xk], writes=[("ln6", i)])
        mv = scr2[:, 20:22]
        p.add("vector",
              lambda e: e.bn_aggr(mv, ln6[:, 0:6 * nst].rearrange(
                  "p (n s) -> p n s", s=6)),
              reads=[("ln6", i) for i in range(nst)], writes=[("mv",)])
        ve = scr2[:, 22:23]
        p.add("vector", lambda e: e.tensor_scalar_add(ve, mv[:, 1:2], cfg.LNEPS),
              reads=[("mv",)], writes=[("ve",)])
        p.add("scalar", lambda e: e.activation(ve, ve, AF.Ln),
              reads=[("ve",)], writes=[("ve",)])
        rstd = scr2[:, 23:24]
        p.add("scalar", lambda e: e.activation(rstd, ve, AF.Exp, scale=-0.5),
              reads=[("ve",)], writes=[("rstd",)])
        nmr = scr2[:, 19:20]
        p.add("vector",
              lambda e: e.scalar_tensor_tensor(nmr, mv[:, 0:1], -1.0, rstd,
                                               ALU.mult, ALU.mult),
              reads=[("mv",), ("rstd",)], writes=[("nmr",)])
        p.add("scalar",
              lambda e: e.activation(out_ap, xa, AF.Identity, bias=nmr,
                                     scale=rstd),
              reads=[xk, ("nmr",), ("rstd",)], writes=[outk])

    def transpose128(dst_ap, dst_key, src_ap, src_key, chan):
        p.add("sync",
              lambda e: e.dma_start(dst_ap, src_ap, transpose=True),
              reads=[src_key], writes=[dst_key], dma_chan=chan, label="tr")

    def rope(ps_ap, ps_key, col0, n, out_ap, out_keys):
        s1, s2, s3 = scr[:, 0, 0:n], scr[:, 1, 0:n], scr[:, 2, 0:n]
        p.add("vector",
              lambda e: e.tensor_mul(s1, ps_ap, cos_sb[:, col0:col0 + n]),
              reads=[ps_key, ("cos",)], writes=[("rs1",)])
        p.add("vector", lambda e: e.stream_shuffle(s2, ps_ap, SHUF_MASK),
              reads=[ps_key], writes=[("rs2",)])
        p.add("vector",
              lambda e: e.tensor_mul(s3, s2, sin_sb[:, col0:col0 + n]),
              reads=[("rs2",), ("sin",)], writes=[("rs3",)])
        p.add("vector", lambda e: e.tensor_add(out_ap, s1, s3),
              reads=[("rs1",), ("rs3",)], writes=list(out_keys))

    def mm(bank_ap, bank_key, lhsT, rhs, reads, start, stop, tp=None,
           skip=False):
        extra = [] if start and not skip else [bank_key]
        return p.add(
            "tensor",
            lambda e: e.matmul(bank_ap, lhsT, rhs, start=start, stop=stop,
                               tile_position=tp, skip_group_check=skip),
            reads=list(reads) + extra, writes=[bank_key])

    def kv256(tab0, kcol0, st0, cb):
        for oc in range(KC):
            bank = pbig if oc % 2 == 0 else pbig2
            bkey = ("pbig",) if oc % 2 == 0 else ("pbig2",)
            for hc in range(HC):
                mm(bank[:, 0:256], bkey,
                   wk_sb[:, hc, 128 * oc:128 * oc + 128],
                   actT[:, hc, cb:cb + 256],
                   list(wk_pg) + [("actT", hc, cb), ("actT", hc, cb + 128)],
                   start=(hc == 0), stop=(hc == HC - 1))
            rope(bank[:, 0:256], bkey, tab0, 256,
                 kT[:, oc, kcol0:kcol0 + 256],
                 [("kT", oc, kcol0), ("kT", oc, kcol0 + 128)])
        for half in range(2):
            bank = pbig if half == 0 else pbig2
            bkey = ("pbig",) if half == 0 else ("pbig2",)
            for hc in range(HC):
                mm(bank[:, 0:KVD], bkey,
                   actT[:, hc, cb + 128 * half:cb + 128 * half + 128],
                   wv_sb[:, hc, :],
                   list(wv_pg) + [("actT", hc, cb + 128 * half)],
                   start=(hc == 0), stop=(hc == HC - 1))
            vv = v_sb[:, :, :].rearrange("p s (a b) -> p s a b",
                                         a=cfg.NKV // 2)
            pvw = bank[:, 0:KVD].rearrange("p (a e d) -> p a e d",
                                           a=cfg.NKV // 2, e=2)
            p.add("scalar",
                  lambda e, pvw=pvw, st=st0 + half, vv=vv: e.copy(
                      vv[:, st, :, 0:64], pvw[:, :, 0, :]),
                  reads=[bkey], writes=[("v", st0 + half)])
            p.add("scalar",
                  lambda e, pvw=pvw, st=st0 + half, vv=vv: e.copy(
                      vv[:, st, :, 65:129], pvw[:, :, 1, :]),
                  reads=[bkey], writes=[("v", st0 + half)])

    def ln_and_transpose(xsrc, row0, dst_cb, chan_i):
        pp = (row0 // 128) % 2
        xa = x_buf[:, pp, :]
        xk = ("xbuf", pp)
        d("gpsimd", xa, xsrc[row0:row0 + 128, :], "D:x%d" % pp, writes=[xk])
        layer_norm(xk, xa, ("xn",), xn_bf[:, :])
        for hc in range(HC):
            bank = P256[hc % 2]
            bview = bank.bitcast(BF16)
            bkey = ("p256", hc % 2)
            p.add("tensor",
                  lambda e, hc=hc, bview=bview: e.transpose(
                      bview[:, 0:128], xn_bf[:, 128 * hc:128 * hc + 128],
                      id_bf[:, 0:128]),
                  reads=[("xn",), ("id",)], writes=[bkey])
            p.add("vector",
                  lambda e, hc=hc, dst_cb=dst_cb, bview=bview: e.tensor_copy(
                      actT[:, hc, dst_cb:dst_cb + 128], bview[:, 0:128]),
                  reads=[bkey], writes=[("actT", hc, dst_cb)])

    d("gpsimd", wk_sb, wk_in[0], "D:wk", writes=list(wk_pg))
    d("gpsimd", wv_sb, wv_in[0], "D:wv", writes=list(wv_pg))
    for ot in range(NSLOT):
        cb = 256 * (ot % 2)
        for half in range(2):
            ln_and_transpose(x_oth, 256 * ot + 128 * half, cb + 128 * half,
                             2 * ot + half)
        kv256(OWN + 256 * ot, OWN + 256 * ot, OWN // 128 + 2 * ot, cb)

    NQ = int(os.environ.get("KK_NQ", cfg.NQ))
    KK_PH = int(os.environ.get("KK_PH", 9))
    for q in range(NQ):
        if q > 0:
            d("gpsimd", wk_sb, wk_in[0], "D:wk", writes=list(wk_pg))
            d("gpsimd", wv_sb, wv_in[0], "D:wv", writes=list(wv_pg))
        for g2 in range(2):
            for half in range(2):
                ln_and_transpose(x_own, 512 * q + 256 * g2 + 128 * half,
                                 256 * g2 + 128 * half, 2 * g2 + half)
            kv256(512 * q + 256 * g2, 512 * q + 256 * g2,
                  4 * q + 2 * g2, 256 * g2)

        if KK_PH < 2:
            continue
        d("gpsimd", wr_sb, wr_in[0], "D:wr", writes=list(wr_pg))
        for tt in range(4):
            bank = P256[tt % 2]
            bkey = ("p256", tt % 2)
            for hc in range(HC):
                mm(bank[:, 0:E], bkey, actT[:, hc, 128 * tt:128 * tt + 128],
                   wr_sb[:, hc, :],
                   list(wr_pg) + [("actT", hc, 128 * tt)],
                   start=(hc == 0), stop=(hc == HC - 1))
            nm = scr2[:, 48:49]
            p.add("vector",
                  lambda e, bank=bank: e.reduce_max(nm, bank[:, 0:E],
                                                    axis=AX.X, negate=True),
                  reads=[bkey], writes=[("nm",)])
            el = scr[:, 0, 0:E]
            Z = scr2[:, 49:50]
            p.add("scalar",
                  lambda e, bank=bank: e.activation(el, bank[:, 0:E], AF.Exp,
                                                    bias=nm, accum_out=Z),
                  reads=[bkey, ("nm",)], writes=[("rs1",), ("Z",)])
            mx = scr2[:, 53:61]
            p.add("vector", lambda e: e.max(out=mx, in_=el),
                  reads=[("rs1",)], writes=[("mx8",)])
            p.add("vector", lambda e: e.memset(scr2[:, 57:61], 0.0),
                  reads=[("mx8",)], writes=[("mx8",)])
            zap = scr[:, 1, 0:E]
            p.add("vector",
                  lambda e: e.match_replace(out=zap, in_to_replace=mx,
                                            in_values=el, imm_value=0.0),
                  reads=[("rs1",), ("mx8",)], writes=[("rs2",)])
            S = scr2[:, 50:51]
            p.add("vector",
                  lambda e: e.scalar_tensor_tensor(g_bf[:, :], zap, -1.0, el,
                                                   ALU.mult, ALU.add,
                                                   accum_out=S),
                  reads=[("rs1",), ("rs2",)], writes=[("gbf",), ("S",)])
            den = scr2[:, 51:52]
            p.add("vector",
                  lambda e: e.scalar_tensor_tensor(den, Z, cfg.QEPS, S,
                                                   ALU.mult, ALU.add),
                  reads=[("Z",), ("S",)], writes=[("den",)])
            rec = scr2[:, 52:53]
            p.add("vector", lambda e: e.reciprocal(rec, den),
                  reads=[("den",)], writes=[("rec",)])
            transpose128(gT_bf[:, :], ("gT",), g_bf[:, :], ("gbf",),
                         "D:tr%d" % (tt % 2))
            cbank = P256[2]
            mm(cbank[:, 0:KE], ("p256", 2), gT_bf[:, :], amat[:, :],
               [("gT",), ("amat",)], start=True, stop=True)
            p.add("vector",
                  lambda e, tt=tt, cbank=cbank: e.tensor_scalar(
                      c_sb[:, tt, :], cbank[:, 0:KE], rec, None, ALU.mult),
                  reads=[("p256", 2), ("rec",)], writes=[("c", tt)])

        if KK_PH < 3:
            continue
        nsub = (IE // 2) // CS
        steps = [(ih, sub, ke) for ih in range(2) for sub in range(nsub)
                 for ke in range(KE)]

        def gu_slab_idx(ih, sub, ke):
            return (ke * IE + ih * (IE // 2) + sub * CS) // CS

        def load_gu(si, buf):
            sl = gu_slab_idx(*steps[si])
            g_ap, g_pg = wgu[0][buf]
            u_ap, u_pg = wgu[1][buf]
            d("gpsimd", g_ap, wg_in[sl], "D:wg%d" % buf, writes=list(g_pg))
            d("gpsimd", u_ap, wu_in[sl], "D:wu%d" % buf, writes=list(u_pg))

        load_gu(0, 0)
        for si, (ih, sub, ke) in enumerate(steps):
            buf = si % 2
            if si + 1 < len(steps):
                load_gu(si + 1, (si + 1) % 2)
            g_ap, gpg = wgu[0][buf]
            u_ap, upg = wgu[1][buf]
            for tt in range(4):
                pg = P256[(si * 4 + tt) % 2]
                pgk = ("p256", (si * 4 + tt) % 2)
                pu = P256[2 + (si * 4 + tt) % 2]
                puk = ("p256", 2 + (si * 4 + tt) % 2)
                for hc in range(HC):
                    mm(pg[:, :], pgk, actT[:, hc, 128 * tt:128 * tt + 128],
                       g_ap[:, hc, :],
                       list(gpg) + [("actT", hc, 128 * tt)],
                       start=(hc == 0), stop=(hc == HC - 1))
                for hc in range(HC):
                    mm(pu[:, :], puk, actT[:, hc, 128 * tt:128 * tt + 128],
                       u_ap[:, hc, :],
                       list(upg) + [("actT", hc, 128 * tt)],
                       start=(hc == 0), stop=(hc == HC - 1))
                pp2 = tt % 2
                p.add("scalar",
                      lambda e, pg=pg, pp2=pp2: e.activation(
                          sg_bf[:, pp2, :], pg[:, :], AF.Silu),
                      reads=[pgk], writes=[("sg", pp2)])
                p.add("vector",
                      lambda e, pu=pu, pp2=pp2: e.tensor_copy(
                          su_bf[:, pp2, :], pu[:, :]),
                      reads=[puk], writes=[("su", pp2)])
                p.add("vector",
                      lambda e, pp2=pp2: e.tensor_mul(
                          bas_bf[:, pp2, :], sg_bf[:, pp2, :],
                          su_bf[:, pp2, :]),
                      reads=[("sg", pp2), ("su", pp2)], writes=[("bas", pp2)])
                yk = ("ya", tt)
                if ke == 0:
                    p.add("vector",
                          lambda e, pp2=pp2, tt=tt, ke=ke: e.tensor_scalar(
                              y_acc[:, tt, :], bas_bf[:, pp2, :],
                              c_sb[:, tt, ke:ke + 1], None, ALU.mult),
                          reads=[("bas", pp2), ("c", tt)], writes=[yk])
                elif ke < KE - 1:
                    p.add("vector",
                          lambda e, pp2=pp2, tt=tt, ke=ke:
                          e.scalar_tensor_tensor(
                              y_acc[:, tt, :], bas_bf[:, pp2, :],
                              c_sb[:, tt, ke:ke + 1],
                              y_acc[:, tt, :], ALU.mult, ALU.add),
                          reads=[("bas", pp2), ("c", tt), yk], writes=[yk])
                else:
                    p.add("vector",
                          lambda e, pp2=pp2, tt=tt, ke=ke:
                          e.scalar_tensor_tensor(
                              ybf_s[:, tt, :], bas_bf[:, pp2, :],
                              c_sb[:, tt, ke:ke + 1],
                              y_acc[:, tt, :], ALU.mult, ALU.add),
                          reads=[("bas", pp2), ("c", tt), yk],
                          writes=[("ybs", tt)])
            if ke == KE - 1:
                for tt in range(4):
                    for ib in range(CS // 128):
                        icol = ih * (IE // 2) + sub * CS + 128 * ib
                        ic = icol // 128
                        transpose128(
                            yT[:, ic, 128 * tt:128 * tt + 128],
                            ("yT", ic, 128 * tt),
                            ybf_s[:, tt, 128 * ib:128 * ib + 128],
                            ("ybs", tt), "D:tr%d" % (tt % 2))

        if KK_PH < 4:
            continue
        nsh = cfg.SHC // CS

        def load_sh(si, buf):
            g_ap, g_pg = wgu[0][buf]
            u_ap, u_pg = wgu[1][buf]
            d("gpsimd", g_ap, swg_in[si], "D:wg%d" % buf, writes=list(g_pg))
            d("gpsimd", u_ap, swu_in[si], "D:wu%d" % buf, writes=list(u_pg))

        load_sh(0, 0)
        for si in range(nsh):
            buf = si % 2
            if si + 1 < nsh:
                load_sh(si + 1, (si + 1) % 2)
            g_ap, gpg = wgu[0][buf]
            u_ap, upg = wgu[1][buf]
            for tt in range(4):
                pg = P256[(si * 4 + tt) % 2]
                pgk = ("p256", (si * 4 + tt) % 2)
                pu = P256[2 + (si * 4 + tt) % 2]
                puk = ("p256", 2 + (si * 4 + tt) % 2)
                for hc in range(HC):
                    mm(pg[:, :], pgk, actT[:, hc, 128 * tt:128 * tt + 128],
                       g_ap[:, hc, :],
                       list(gpg) + [("actT", hc, 128 * tt)],
                       start=(hc == 0), stop=(hc == HC - 1))
                for hc in range(HC):
                    mm(pu[:, :], puk, actT[:, hc, 128 * tt:128 * tt + 128],
                       u_ap[:, hc, :],
                       list(upg) + [("actT", hc, 128 * tt)],
                       start=(hc == 0), stop=(hc == HC - 1))
                pp2 = tt % 2
                p.add("scalar",
                      lambda e, pg=pg, pp2=pp2: e.activation(
                          sg_bf[:, pp2, :], pg[:, :], AF.Silu),
                      reads=[pgk], writes=[("sg", pp2)])
                p.add("vector",
                      lambda e, pu=pu, pp2=pp2: e.tensor_copy(
                          su_bf[:, pp2, :], pu[:, :]),
                      reads=[puk], writes=[("su", pp2)])
                p.add("vector",
                      lambda e, pp2=pp2: e.tensor_mul(
                          sh_bf[:, pp2, :], sg_bf[:, pp2, :], su_bf[:, pp2, :]),
                      reads=[("sg", pp2), ("su", pp2)], writes=[("bas", pp2)])
                for ib in range(CS // 128):
                    ic = (si * CS + 128 * ib) // 128
                    transpose128(shT[:, ic, 128 * tt:128 * tt + 128],
                                 ("shT", ic, 128 * tt),
                                 sh_bf[:, pp2, 128 * ib:128 * ib + 128],
                                 ("bas", pp2), "D:tr%d" % (tt % 2))

        if KK_PH < 5:
            continue
        def load_wq(si, buf):
            q_ap, q_pg = wq_b[buf]
            d("gpsimd", q_ap, wq_in[si], "D:wq%d" % buf, writes=list(q_pg))

        load_wq(0, 0)
        for si in range(HC // 2):
            buf = si % 2
            if si + 1 < HC // 2:
                load_wq(si + 1, (si + 1) % 2)
            q_ap, q_pg = wq_b[buf]
            for oc2 in range(2):
                oc = 2 * si + oc2
                bank = pbig if oc % 2 == 0 else pbig2
                bkey = ("pbig",) if oc % 2 == 0 else ("pbig2",)
                for hc in range(HC):
                    mm(bank[:, 0:512], bkey,
                       q_ap[:, hc, 128 * oc2:128 * oc2 + 128],
                       actT[:, hc, :],
                       list(q_pg) + [("actT", hc, o)
                                     for o in (0, 128, 256, 384)],
                       start=(hc == 0), stop=(hc == HC - 1))
                for hq in range(2):
                    rope(bank[:, 256 * hq:256 * hq + 256], bkey,
                         512 * q + 256 * hq, 256,
                         qT_v[:, oc, 256 * hq:256 * hq + 256],
                         [("qT", oc, 256 * hq)] + list(qT_pg))

        if KK_PH < 6:
            continue
        KK_NC = int(os.environ.get("KK_NC", 99))
        KK_TR = int(os.environ.get("KK_TR", 99))
        KK_NORM = int(os.environ.get("KK_NORM", 1))
        KK_EXP = int(os.environ.get("KK_EXP", 1))
        KK_PV = int(os.environ.get("KK_PV", 1))
        KK_SUM = int(os.environ.get("KK_SUM", 1))
        for sl in range(2):
            j = 2 * q + sl
            qc0 = 256 * sl
            ntr = min(cfg.trips(j), KK_TR)
            def emit_zq(c, hf):
                for ii in range(2):
                    qch = 4 * c + 2 * hf + ii
                    p.add("vector",
                          lambda e, qch=qch, ii=ii, qc0=qc0: e.tensor_copy(
                              zq[0:64, 2 * ii, :],
                              qT_v[0:64, qch, qc0:qc0 + 256]),
                          reads=[("qT", qch, qc0), qT_pg[0]],
                          writes=[("zq", 2 * ii)])
                    p.add("vector",
                          lambda e, qch=qch, ii=ii, qc0=qc0: e.tensor_copy(
                              zq[64:128, 2 * ii + 1, :],
                              qT_v[64:128, qch, qc0:qc0 + 256]),
                          reads=[("qT", qch, qc0), qT_pg[0]],
                          writes=[("zq", 2 * ii + 1)])

            def emit_scores(j, sc, cc):
                if sc < 2 * j + 2:
                    kcol = 128 * sc
                else:
                    kcol = OWN + 128 * (sc - (2 * j + 2))
                diag = (sc == 2 * j) or (sc == 2 * j + 1)
                sreg = pbig if sc % 2 == 0 else pbig2
                srk = ("pbig",) if sc % 2 == 0 else ("pbig2",)
                for ii in range(2):
                    mm(sreg[:, 512 * ii:512 * ii + 256], srk,
                       kT[:, cc, kcol:kcol + 128], zq[:, 2 * ii, :],
                       [("kT", cc, kcol), ("zq", 2 * ii)],
                       start=True, stop=False)
                    mm(sreg[:, 512 * ii + 256:512 * ii + 512], srk,
                       kT[:, cc, kcol:kcol + 128], zq[:, 2 * ii + 1, :],
                       [("kT", cc, kcol), ("zq", 2 * ii + 1)],
                       start=False, stop=True)
                pp3 = sc % 3
                bcol = bias_sb[:, cfg.bias_col(j, sc):cfg.bias_col(j, sc) + 1]
                if not KK_EXP:
                    return
                if not diag:
                    p.add("scalar",
                          lambda e, sreg=sreg, pp3=pp3, bcol=bcol:
                          e.activation(exp_bf[:, pp3, :], sreg[:, :],
                                       AF.Exp, bias=bcol, scale=0.125),
                          reads=[srk, ("bias",)], writes=[("exp", pp3)])
                else:
                    dd = sc - 2 * j
                    if dd == 0:
                        exd, exdk = expd_bf, [("rs1",), ("rs2",)]
                    else:
                        exd = y_acc[:, 0:2, :].rearrange(
                            "p a b -> p (a b)").bitcast(BF16)
                        exdk = [("ya", 0), ("ya", 1)]
                    p.add("scalar",
                          lambda e, sreg=sreg, bcol=bcol, exd=exd:
                          e.activation(exd, sreg[:, :],
                                       AF.Exp, bias=bcol, scale=0.125),
                          reads=[srk, ("bias",)], writes=exdk)
                    p.add("vector",
                          lambda e, pp3=pp3, dd=dd, exd=exd: e.tensor_mul(
                              exp_bf[:, pp3, :].rearrange(
                                  "p (a b) -> p a b", a=4),
                              exd.rearrange("p (a b) -> p a b", a=4),
                              tri_sb[:, dd:dd + 1, :].to_broadcast(
                                  [128, 4, 256])),
                          reads=exdk + [("tri",)],
                          writes=[("exp", pp3)])

            def emit_pv(sc, ntr, st):
                if not (KK_EXP and KK_PV):
                    return
                pp3 = sc % 3
                ek = ("exp", pp3)
                first = (sc == 0)
                last = (sc == ntr - 1)
                for ii in range(2):
                    ex_lo = exp_bf[:, pp3, 512 * ii:512 * ii + 256]
                    ex_hi = exp_bf[:, pp3, 512 * ii + 256:512 * ii + 512]
                    mm(P256[2 * ii][:, :], ("p256", 2 * ii),
                       v_sb[:, st, 129 * c:129 * c + 128],
                       ex_lo, [("v", st), ("vones",), ek],
                       start=first, stop=last)
                    mm(P256[2 * ii + 1][:, :], ("p256", 2 * ii + 1),
                       v_sb[:, st, 129 * c + 1:129 * c + 129],
                       ex_hi, [("v", st), ("vones",), ek],
                       start=first, stop=last)

            passes = [(c, hf) for c in range(min(KC, KK_NC)) for hf in range(2)]
            emit_zq(*passes[0])
            pre = 0
            for pi, (c, hf) in enumerate(passes):
                pend = []
                for sc in range(ntr):
                    st = sc if sc < 2 * j + 2 else OWN // 128 + (sc - (2 * j + 2))
                    if sc >= pre:
                        emit_scores(j, sc, c)
                    while len(pend) >= 2:
                        args = pend.pop(0)
                        emit_pv(*args)
                    pend.append((sc, ntr, st))
                for args in pend:
                    emit_pv(*args)
                pend = []
                if pi + 1 < len(passes):
                    emit_zq(*passes[pi + 1])
                    pre = min(2, ntr)
                    for sc2 in range(pre):
                        emit_scores(j, sc2, passes[pi + 1][0])
                else:
                    pre = 0
                RW = [("rcpb",), ("rcpo",), ("rcpo2",), ("ybs", 0), ("ybs", 1)]
                binfo = []
                for ii in range(2 if KK_NORM else 0):
                    be, bo = P256[2 * ii], P256[2 * ii + 1]
                    bek, bok = ("p256", 2 * ii), ("p256", 2 * ii + 1)
                    binfo.append((be, bek, bo, bok, 4 * c + 2 * hf + ii))
                    p.add("scalar",
                          lambda e, be=be, ii=ii: e.activation(
                              scr[64:65, 1 + ii, :], be[64:65, :], AF.Ln),
                          reads=[bek], writes=[("rs2",)])
                    p.add("scalar",
                          lambda e, bo=bo, ii=ii: e.activation(
                              scr[32:64, 1 + ii, :], bo[32:64, :], AF.Ln),
                          reads=[bok], writes=[("rs2",)])
                if KK_NORM:
                    p.add("scalar",
                          lambda e: e.activation(
                              ybf_s[64:65, 0:2, :], scr[64:65, 1:3, :],
                              AF.Exp, scale=-1.0),
                          reads=[("rs2",)],
                          writes=[("rcpb",), ("ybs", 0), ("ybs", 1)])
                    p.add("scalar",
                          lambda e: e.activation(
                              ybf_s[32:64, 0:2, :], scr[32:64, 1:3, :],
                              AF.Exp, scale=-1.0),
                          reads=[("rs2",)],
                          writes=[("rcpo",), ("rcpo2",)])
                    p.add("vector",
                          lambda e: e.memset(ybf_s[32:63, 0:2, :], 0.0),
                          reads=[("rcpo",)], writes=[("rcpo",)])
                    mm(pbig[:, 0:512], ("pbig",), id_bf[:, 128:256],
                       ybf_s[:, 0:2, :].rearrange("p a b -> p (a b)"),
                       [("id",)] + RW, start=True, stop=True)
                    p.add("vector",
                          lambda e: e.tensor_copy(
                              scr[:, 0:2, :].rearrange("p a b -> p (a b)"),
                              pbig[:, 0:512]),
                          reads=[("pbig",)], writes=[("rs1",), ("rs2",)])
                for ii, (be, bek, bo, bok, qch) in enumerate(binfo):
                    p.add("vector",
                          lambda e, be=be, qch=qch, qc0=qc0, ii=ii:
                          e.tensor_mul(
                              actT[0:64, qch, qc0:qc0 + 256],
                              be[0:64, :], scr[0:64, ii, :]),
                          reads=[bek, ("rs1",), ("rs2",)],
                          writes=[("actT", qch, qc0),
                                  ("actT", qch, qc0 + 128)])
                    p.add("vector",
                          lambda e, bo=bo, qch=qch, qc0=qc0, ii=ii:
                          e.tensor_mul(
                              actT[64:128, qch, qc0:qc0 + 256],
                              bo[64:128, :], scr[64:128, ii, :]),
                          reads=[bok, ("rs1",), ("rs2",)],
                          writes=[("actT", qch, qc0),
                                  ("actT", qch, qc0 + 128)])

        def load_b6(osi, buf):
            (o_ap, o_pg), (d_ap, d_pg), (s_ap, s_pg) = b6b[buf]
            d("gpsimd", o_ap, wo_in[osi], "D:b6o%d" % buf, writes=list(o_pg))
            d("gpsimd", d_ap, wd_in[osi], "D:b6d%d" % buf, writes=list(d_pg))
            d("gpsimd", s_ap, swd_in[osi], "D:b6s%d" % buf, writes=list(s_pg))

        nos = H // OS
        load_b6(0, 0)
        for osi in range(nos):
            buf = osi % 2
            if osi + 1 < nos:
                load_b6(osi + 1, (osi + 1) % 2)
            (o_ap, o_pg), (d_ap, d_pg), (s_ap, s_pg) = b6b[buf]
            for tt in range(4):
                bank = P256[(osi * 4 + tt) % 4]
                bkey = ("p256", (osi * 4 + tt) % 4)
                nmm = HC + NIC + NSC
                k_ = 0
                for hc in range(HC):
                    mm(bank[:, :], bkey, actT[:, hc, 128 * tt:128 * tt + 128],
                       o_ap[:, hc, :],
                       list(o_pg) + [("actT", hc, 128 * tt)],
                       start=(k_ == 0), stop=False)
                    k_ += 1
                for ic in range(NIC):
                    mm(bank[:, :], bkey, yT[:, ic, 128 * tt:128 * tt + 128],
                       d_ap[:, ic, :],
                       list(d_pg) + [("yT", ic, 128 * tt)],
                       start=False, stop=False)
                    k_ += 1
                for ic in range(NSC):
                    mm(bank[:, :], bkey, shT[:, ic, 128 * tt:128 * tt + 128],
                       s_ap[:, ic, :],
                       list(s_pg) + [("shT", ic, 128 * tt)],
                       start=False, stop=(ic == NSC - 1))
                    k_ += 1
                ppx = tt % 2
                xsl = xslice[:, ppx, :]
                xslk = ("xs", ppx)
                row0 = 512 * q + 128 * tt
                d("sync", xsl, x_own[row0:row0 + 128, OS * osi:OS * osi + OS],
                  "D:xs%d" % ppx, writes=[xslk])
                o_st = ost[:, ppx, :]
                ok = ("ost", ppx)
                p.add("vector",
                      lambda e, bank=bank, xsl=xsl, o_st=o_st: e.tensor_add(
                          o_st, bank[:, :], xsl),
                      reads=[bkey, xslk], writes=[ok])
                d("sync", out_d[row0:row0 + 128, OS * osi:OS * osi + OS],
                  o_st, "D:out", reads=[ok])

    with nc.allow_low_precision(reason="bf16 matmul operand staging"):
        with nc.Block() as block:
            p.emit(block)
    for t in reversed(ctx):
        t.__exit__(None, None, None)
    return nc, p


def kernel(**inputs):
    cfg = Cfg(tiny=False)
    shared, per_core = host_prep(cfg, inputs)
    nc, _ = build_nc(cfg)
    in_maps = []
    for core in range(cfg.NCORE):
        m = {"x_own": per_core[core]["x_own"],
             "x_oth": per_core[core]["x_oth"],
             "rcos": per_core[core]["rcos"],
             "rsin": per_core[core]["rsin"],
             "biasm": per_core[core]["biasm"],
             "tri": shared["tri"], "amps": shared["amps"],
             "escale": shared["escale"], "ident": shared["ident"],
             "wq": shared["wq"],
             "wk": shared["wk"], "wv": shared["wv"], "wo": shared["wo"],
             "wr": shared["wr"], "wg": shared["wg"], "wu": shared["wu"],
             "wd": shared["wd"], "swg": shared["swg"], "swu": shared["swu"],
             "swd": shared["swd"]}
        in_maps.append(m)
    res = run_bass_kernel_spmd(nc, in_maps, core_ids=list(range(cfg.NCORE)))
    out = np.zeros((cfg.B, cfg.T, cfg.H), np.float32)
    for core in range(cfg.NCORE):
        seq, par = core // 2, core % 2
        o = res.results[core]["out"].reshape(cfg.NSLOT, cfg.SLOT, cfg.H)
        for j in range(cfg.NSLOT):
            t = 2 * j + par
            out[seq, cfg.SLOT * t:cfg.SLOT * (t + 1), :] = o[j]
    return out


# version 16
# speedup vs baseline: 1.0279x; 1.0279x over previous
import os
import sys
sys.path.insert(0, "/opt/trn_rl_repo")
import numpy as np
import ml_dtypes
import concourse.bass as bass
import concourse.mybir as mybir
from concourse.bass_utils import run_bass_kernel_spmd

F32 = mybir.dt.float32
BF16 = mybir.dt.bfloat16
AF = mybir.ActivationFunctionType
ALU = mybir.AluOpType
AX = mybir.AxisListType

ENGS = ("tensor", "vector", "scalar", "gpsimd", "sync")
SHUF_MASK = []
for _i in range(16):
    SHUF_MASK += [2 * _i + 1, 2 * _i]


class Cfg:
    def __init__(self, tiny=False):
        if tiny:
            self.B, self.T, self.H = 1, 1024, 512
            self.NH, self.NKV = 8, 2
            self.E, self.KE, self.IE = 128, 8, 512
            self.ISH = 256
        else:
            self.B, self.T, self.H = 4, 4096, 2048
            self.NH, self.NKV = 32, 8
            self.E, self.KE, self.IE = 128, 8, 1024
            self.ISH = 1024
        self.DH = 64
        self.HC = self.H // 128
        self.SLOT = 256
        self.NSLOT = self.T // (2 * self.SLOT)
        self.NQ = self.NSLOT // 2
        self.NCORE = 2 * self.B
        self.KV = self.T
        self.NKT = self.T // 128
        self.OWN = self.NSLOT * self.SLOT
        self.KC = self.NKV * self.DH // 128
        self.GCOL = self.KE * self.IE
        self.SHC = 2 * self.ISH
        self.CS = 256
        self.OS = 256
        self.QEPS = 1e-8
        self.LNEPS = 1e-5
        self.EXPB = -30.0
        self.NEGB = -330.0

    def trips(self, j):
        return 4 * j + 4

    def bias_col(self, j, sc):
        return sum(self.trips(jj) for jj in range(j)) + sc

    @property
    def nbias(self):
        return sum(self.trips(j) for j in range(self.NSLOT))


class Op:
    __slots__ = ("eng", "fn", "idx", "waits", "inc", "is_dma", "label")

    def __init__(self, eng, fn, idx, is_dma, label):
        self.eng, self.fn, self.idx = eng, fn, idx
        self.waits = []
        self.inc = None
        self.is_dma = is_dma
        self.label = label


class PB:
    def __init__(self, nc):
        self.nc = nc
        self.ops = {e: [] for e in ENGS}
        self.sems = {}
        self.eng_inc_count = {e: 0 for e in ENGS}
        self.eng_incs = {e: [] for e in ENGS}
        self.chan_total = {}
        self.last_writer = {}
        self.readers = {}
        self.clock = {e: {} for e in ENGS}
        self.n_waits = 0

    def sem(self, name):
        if name not in self.sems:
            self.sems[name] = self.nc.alloc_semaphore(name.replace(":", "_"))
        return self.sems[name]

    def _resolve_eng_dep(self, eng, idx):
        incs = self.eng_incs[eng]
        lo, hi = 0, len(incs)
        while lo < hi:
            mid = (lo + hi) // 2
            if incs[mid][0] >= idx:
                hi = mid
            else:
                lo = mid + 1
        if lo < len(incs):
            return ("E:" + eng, incs[lo][1])
        op = self.ops[eng][idx]
        assert op.inc is None
        self.eng_inc_count[eng] += 1
        val = self.eng_inc_count[eng]
        op.inc = ("E:" + eng, val)
        incs.append((idx, val))
        return ("E:" + eng, val)

    def _need(self, me_eng, waits, dep):
        deng, didx = dep
        op = self.ops[deng][didx]
        if op.is_dma:
            sem = op.inc[0]
            val = self.chan_total[sem]
        else:
            sem, val = self._resolve_eng_dep(deng, didx)
        myc = self.clock[me_eng]
        if myc.get(sem, 0) < val:
            waits[sem] = max(waits.get(sem, 0), val)
            myc[sem] = val

    def add(self, eng, fn, reads=(), writes=(), dma_chan=None, label=""):
        idx = len(self.ops[eng])
        is_dma = dma_chan is not None
        op = Op(eng, fn, idx, is_dma, label)
        self.ops[eng].append(op)
        waits = {}
        deps = []
        for k in reads:
            w = self.last_writer.get(k)
            if w is not None:
                deps.append(w)
        for k in writes:
            w = self.last_writer.get(k)
            if w is not None:
                deps.append(w)
            deps.extend(self.readers.get(k, ()))
        me = (eng, idx)
        for dep in deps:
            if dep == me:
                continue
            if eng == "tensor" and dep[0] == "tensor":
                continue
            self._need(eng, waits, dep)
        op.waits = sorted(waits.items())
        self.n_waits += len(op.waits)
        if is_dma:
            tot = self.chan_total.get(dma_chan, 0) + 16
            self.chan_total[dma_chan] = tot
            op.inc = (dma_chan, tot)
        for k in writes:
            self.last_writer[k] = me
            self.readers[k] = []
        for k in reads:
            self.readers.setdefault(k, []).append(me)
        return me

    def emit(self, block, final_eng="sync"):
        for e in ENGS:
            if self.eng_inc_count[e]:
                self.sem("E:" + e)
        for chan in sorted(self.chan_total):
            self.sem(chan)
        engines = {"tensor": block.tensor, "vector": block.vector,
                   "scalar": block.scalar, "gpsimd": block.gpsimd,
                   "sync": block.sync}

        def mk(eng_name):
            def section(eng):
                for op in self.ops[eng_name]:
                    for sname, val in op.waits:
                        eng.wait_ge(self.sems[sname], val)
                    inst = op.fn(eng)
                    if op.inc is not None:
                        inst.then_inc(self.sems[op.inc[0]],
                                      16 if op.is_dma else 1)
                if eng_name == final_eng:
                    for chan, tot in sorted(self.chan_total.items()):
                        eng.wait_ge(self.sems[chan], tot)
                    for e in ENGS:
                        if e != eng_name and self.eng_inc_count[e]:
                            eng.wait_ge(self.sems["E:" + e], self.eng_inc_count[e])
            return section

        for name, dec in engines.items():
            dec(mk(name))


def host_prep(cfg, inputs):
    f32 = np.float32
    H = cfg.H
    Wq = np.asarray(inputs["L0_attn_Wq"], f32)
    Wk = np.asarray(inputs["L0_attn_Wk"], f32)
    Wv = np.asarray(inputs["L0_attn_Wv"], f32)
    Wo = np.asarray(inputs["L0_attn_Wo"], f32)
    w_router = np.asarray(inputs["L0_ffn_w_router"], f32)
    w_gate = np.asarray(inputs["L0_ffn_w_gate"], f32)
    w_up = np.asarray(inputs["L0_ffn_w_up"], f32)
    w_down = np.asarray(inputs["L0_ffn_w_down"], f32)
    amps = np.asarray(inputs["L0_ffn_amplitudes"], f32)
    escale = np.asarray(inputs["L0_ffn_expert_scale"], f32)
    swg = np.asarray(inputs["L0_ffn_shared_w_gate"], f32)
    swu = np.asarray(inputs["L0_ffn_shared_w_up"], f32)
    swd = np.asarray(inputs["L0_ffn_shared_w_down"], f32)
    x = np.asarray(inputs["L0_attn_x"], f32)

    qperm = []
    for c in range(cfg.NKV // 2):
        for i in range(4):
            for h in (8 * c + i, 8 * c + 4 + i):
                qperm.extend(range(64 * h, 64 * h + 64))
    qperm = np.array(qperm)

    def slab(w, cs):
        K, C = w.shape
        return np.ascontiguousarray(
            w.reshape(K // 128, 128, C // cs, cs).transpose(2, 1, 0, 3))

    shared = {
        "wq": slab(np.ascontiguousarray(Wq.T)[:, qperm], 256),
        "wk": slab(np.ascontiguousarray(Wk.T), Wk.shape[0]),
        "wv": slab(np.ascontiguousarray(Wv.T), Wv.shape[0]),
        "wo": slab(np.ascontiguousarray(Wo.T)[qperm, :], cfg.OS),
        "wr": slab(w_router, cfg.E),
        "wg": slab(w_gate, cfg.CS),
        "wu": slab(w_up, cfg.CS),
        "wd": slab(w_down, cfg.OS),
        "swg": slab(swg, cfg.CS),
        "swu": slab(swu, cfg.CS),
        "swd": slab(swd, cfg.OS),
        "amps": np.ascontiguousarray(amps.reshape(cfg.E, 2 * cfg.KE)),
        "escale": np.ascontiguousarray(escale.reshape(cfg.E, 1)),
    }
    shared_ident = np.eye(128, dtype=f32)
    tri = np.zeros((128, 2, cfg.SLOT), f32)
    for dd in range(2):
        tri[:, dd, :] = (128 * dd + np.arange(128)[:, None]
                         <= np.arange(cfg.SLOT)[None, :])
    shared["tri"] = tri
    shared["ident"] = shared_ident

    invf = 1.0 / 10000.0 ** (np.arange(0, cfg.DH, 2, dtype=f32) / cfg.DH)
    didx = np.arange(cfg.DH)
    ang_i = didx % (cfg.DH // 2)
    sign = np.where(didx % 2 == 0, -1.0, 1.0).astype(f32)

    per_core = []
    for core in range(cfg.NCORE):
        seq, par = core // 2, core % 2
        t_own = [2 * j + par for j in range(cfg.NSLOT)]
        t_oth = [2 * j + (1 - par) for j in range(cfg.NSLOT)]
        xs = x[seq].reshape(2 * cfg.NSLOT, cfg.SLOT, H)
        x_own = np.ascontiguousarray(xs[t_own].reshape(cfg.OWN, H))
        x_oth = np.ascontiguousarray(xs[t_oth].reshape(cfg.OWN, H))
        pos = np.concatenate(
            [cfg.SLOT * t + np.arange(cfg.SLOT) for t in t_own] +
            [cfg.SLOT * t + np.arange(cfg.SLOT) for t in t_oth]).astype(f32)
        ang = pos[None, :] * invf[ang_i][:, None]
        ctab = np.ascontiguousarray(
            np.tile(np.cos(ang), (2, 1)).astype(ml_dtypes.bfloat16))
        stab = np.ascontiguousarray(
            np.tile(np.sin(ang) * sign[:, None],
                    (2, 1)).astype(ml_dtypes.bfloat16))
        biasm = np.full((128, cfg.nbias), cfg.EXPB, f32)
        for j in range(cfg.NSLOT):
            q0 = cfg.SLOT * t_own[j]
            for oc in range(2 * j + 2):
                col = cfg.bias_col(j, 2 * j + 2 + oc)
                kpos = (cfg.SLOT * t_oth[oc // 2] + 128 * (oc % 2)
                        + np.arange(128))
                biasm[:, col] = np.where(kpos < q0, cfg.EXPB, cfg.NEGB)
        per_core.append({"x_own": x_own, "x_oth": x_oth, "rcos": ctab,
                         "rsin": stab, "biasm": biasm})
    return shared, per_core


def build_nc(cfg):
    nc = bass.Bass("TRN2", target_bir_lowering=False)
    H, HC, KC = cfg.H, cfg.HC, cfg.KC
    SLOT, NSLOT, OWN, KV = cfg.SLOT, cfg.NSLOT, cfg.OWN, cfg.KV
    CS, OS, KE, IE, E = cfg.CS, cfg.OS, cfg.KE, cfg.IE, cfg.E
    KVD = cfg.NKV * 64
    NIC = IE // 128
    NSC = cfg.SHC // 128

    def din(n, s, k="ExternalInput"):
        return nc.dram_tensor(n, s, F32, kind=k).ap()

    x_own = din("x_own", [OWN, H])
    x_oth = din("x_oth", [OWN, H])
    rcos = nc.dram_tensor("rcos", [128, 2 * OWN], BF16, kind="ExternalInput").ap()
    rsin = nc.dram_tensor("rsin", [128, 2 * OWN], BF16, kind="ExternalInput").ap()
    biasm = din("biasm", [128, cfg.nbias])
    tri_in = din("tri", [128, 2, SLOT])
    amps_in = din("amps", [E, 2 * KE])
    id_in = din("ident", [128, 128])
    esc_in = din("escale", [E, 1])
    wq_in = din("wq", [H // 256, 128, HC, 256])
    wk_in = din("wk", [1, 128, HC, KVD])
    wv_in = din("wv", [1, 128, HC, KVD])
    wo_in = din("wo", [H // OS, 128, HC, OS])
    wr_in = din("wr", [1, 128, HC, E])
    wg_in = din("wg", [cfg.GCOL // CS, 128, HC, CS])
    wu_in = din("wu", [cfg.GCOL // CS, 128, HC, CS])
    wd_in = din("wd", [H // OS, 128, NIC, OS])
    swg_in = din("swg", [cfg.SHC // CS, 128, HC, CS])
    swu_in = din("swu", [cfg.SHC // CS, 128, HC, CS])
    swd_in = din("swd", [H // OS, 128, NSC, OS])
    out_d = din("out", [OWN, H], k="ExternalOutput")

    ctx = []

    def sb(name, shape, dtype, **kw):
        t = nc.sbuf_tensor(name, shape, dtype, **kw)
        ctx.append(t)
        return t.__enter__()

    def psum(name, shape):
        t = nc.psum_tensor(name, shape, F32)
        ctx.append(t)
        return t.__enter__()

    kT = sb("kT", [128, KC, KV], BF16)
    v_sb = sb("v_sb", [128, cfg.NKT, KVD], BF16)
    cos_sb = sb("cos_sb", [128, 2 * OWN], BF16)
    sin_sb = sb("sin_sb", [128, 2 * OWN], BF16)
    bias_sb = sb("bias_sb", [128, cfg.nbias], F32)
    tri_sb = sb("tri_sb", [128, 2, SLOT], BF16)
    ones_bf = sb("ones_bf", [128, 64], BF16)
    id_bf = sb("id_bf", [128, 128], BF16)
    amat = sb("amat", [128, KE], BF16)
    x_buf = sb("x_buf", [128, 2, H], F32)
    xslice = sb("xslice", [128, 2, OS], F32)
    xn_bf = sb("xn_bf", [128, H], BF16)
    actT = sb("actT", [128, HC, 512], BF16, is_transpose_dest=True)
    shT = sb("shT", [128, NSC, 512], BF16, is_transpose_dest=True)
    yT = sb("yT", [128, NIC, 512], BF16, is_transpose_dest=True)
    y_acc = sb("y_acc", [128, 4, 256], F32)
    ybf_s = sb("ybf_s", [128, 4, 256], BF16)
    sg_bf = sb("sg_bf", [128, 2, CS], BF16)
    su_bf = sb("su_bf", [128, 2, CS], BF16)
    bas_bf = sb("bas_bf", [128, 2, CS], BF16)
    sh_bf = bas_bf
    exp_bf = sb("exp_bf", [128, 3, 1024], BF16)
    g_bf = sb("g_bf", [128, 128], BF16)
    gT_bf = sb("gT_bf", [128, 128], BF16, is_transpose_dest=True)
    c_sb = sb("c_sb", [128, 4, KE], F32)
    zq = sb("zq", [128, 4, 256], BF16)
    scr = sb("scr", [128, 3, 256], F32)
    expd_bf = scr[:, 0:2, :].rearrange("p a b -> p (a b)").bitcast(BF16)
    scr2 = sb("scr2", [128, 64], F32)
    ln6 = sb("ln6", [128, 24], F32)
    ost = sb("ost", [128, 2, OS], F32)
    arena = sb("arena", [128, 20480], BF16)

    def ar(off_kb, shape):
        n = int(np.prod(shape))
        off = off_kb * 512
        v = arena[:, off:off + n]
        if len(shape) == 2:
            v = v.rearrange("p (a b) -> p a b", a=shape[0], b=shape[1])
        size_kb = (2 * n + 1023) // 1024
        pages = tuple(("ar", i) for i in
                      range(off_kb // 4, (off_kb + size_kb + 3) // 4))
        return v, pages

    kb_wkv = (HC * KVD * 2 + 1023) // 1024
    wk_sb, wk_pg = ar(0, (HC, KVD))
    wv_sb, wv_pg = ar(16, (HC, KVD))
    wr_sb, wr_pg = ar(32, (HC, E))
    kb_gu = (HC * CS * 2 + 1023) // 1024
    wgu = []
    for w in range(2):
        wgu.append([ar(kb_gu * b + 2 * kb_gu * w, (HC, CS)) for b in range(2)])
    wq_b = [ar(kb_gu * b, (HC, 256)) for b in range(2)]
    qT_v, qT_pg = ar(24, (HC, 512))
    kb_o = (HC * OS * 2 + 1023) // 1024
    kb_d = (NIC * OS * 2 + 1023) // 1024
    kb_s = (NSC * OS * 2 + 1023) // 1024
    kb_b6 = kb_o + kb_d + kb_s
    b6b = []
    for b in range(2):
        base = kb_b6 * b
        o, opg = ar(base, (HC, OS))
        d_, dpg = ar(base + kb_o, (NIC, OS))
        s_, spg = ar(base + kb_o + kb_d, (NSC, OS))
        b6b.append(((o, opg), (d_, dpg), (s_, spg)))

    pbig = psum("pbig", [128, 1024])
    pbig2 = psum("pbig2", [128, 1024])
    P256 = [psum("p256_%d" % i, [128, 256]) for i in range(4)]

    p = PB(nc)

    def d(eng, dst, src, chan, reads=(), writes=(), label=""):
        return p.add(eng, lambda e, dst=dst, src=src: e.dma_start(dst, src),
                     reads=reads, writes=writes, dma_chan=chan, label=label)

    amp_sb = scr2[:, 24:24 + 2 * KE]
    esc_sb = scr2[:, 40:41]
    d("sync", cos_sb[:, :], rcos, "D:pre", writes=[("cos",)])
    d("sync", sin_sb[:, :], rsin, "D:pre", writes=[("sin",)])
    d("sync", bias_sb[:, :], biasm, "D:pre", writes=[("bias",)])
    d("gpsimd", tri_sb[:, :, :], tri_in, "D:preg", writes=[("tri",)])
    d("gpsimd", id_bf[:, :], id_in, "D:preg", writes=[("id",)])
    d("sync", amp_sb, amps_in, "D:pre", writes=[("amp",)])
    d("sync", esc_sb, esc_in, "D:pre", writes=[("esc",)])
    p.add("vector", lambda e: e.memset(ones_bf[:, :], 1.0), writes=[("ones",)])
    p.add("vector", lambda e: e.memset(zq[:, :, :], 0.0),
          writes=[("zq", v) for v in range(4)])

    a0 = amp_sb.rearrange("p (k two) -> p k two", two=2)[:, :, 0]
    a1 = amp_sb.rearrange("p (k two) -> p k two", two=2)[:, :, 1]
    t0 = scr2[:, 0:KE]
    t1 = scr2[:, 8:16]
    p.add("vector", lambda e: e.tensor_mul(t0, a0, a0),
          reads=[("amp",)], writes=[("t0",)])
    p.add("vector", lambda e: e.tensor_mul(t1, a1, a1),
          reads=[("amp",)], writes=[("t1",)])
    p.add("vector", lambda e: e.tensor_add(t0, t0, t1),
          reads=[("t0",), ("t1",)], writes=[("t0",)])
    ssum = scr2[:, 17:18]
    rsum = scr2[:, 18:19]
    p.add("vector", lambda e: e.reduce_sum(ssum, t0, axis=AX.X),
          reads=[("t0",)], writes=[("ssum",)])
    p.add("vector", lambda e: e.tensor_scalar_add(ssum, ssum, cfg.QEPS),
          reads=[("ssum",)], writes=[("ssum",)])
    p.add("vector", lambda e: e.reciprocal(rsum, ssum),
          reads=[("ssum",)], writes=[("rsum",)])
    p.add("vector", lambda e: e.tensor_scalar(t0, t0, rsum, None, ALU.mult),
          reads=[("t0",), ("rsum",)], writes=[("t0",)])
    p.add("vector", lambda e: e.tensor_scalar(amat[:, :], t0, esc_sb, None,
                                              ALU.mult),
          reads=[("t0",), ("esc",)], writes=[("amat",)])

    def layer_norm(xk, xa, outk, out_ap):
        nst = H // 512
        for i in range(nst):
            p.add("vector",
                  lambda e, i=i: e.bn_stats(ln6[:, 6 * i:6 * i + 6],
                                            xa[:, 512 * i:512 * i + 512]),
                  reads=[xk], writes=[("ln6", i)])
        mv = scr2[:, 20:22]
        p.add("vector",
              lambda e: e.bn_aggr(mv, ln6[:, 0:6 * nst].rearrange(
                  "p (n s) -> p n s", s=6)),
              reads=[("ln6", i) for i in range(nst)], writes=[("mv",)])
        ve = scr2[:, 22:23]
        p.add("vector", lambda e: e.tensor_scalar_add(ve, mv[:, 1:2], cfg.LNEPS),
              reads=[("mv",)], writes=[("ve",)])
        p.add("scalar", lambda e: e.activation(ve, ve, AF.Ln),
              reads=[("ve",)], writes=[("ve",)])
        rstd = scr2[:, 23:24]
        p.add("scalar", lambda e: e.activation(rstd, ve, AF.Exp, scale=-0.5),
              reads=[("ve",)], writes=[("rstd",)])
        nmr = scr2[:, 19:20]
        p.add("vector",
              lambda e: e.scalar_tensor_tensor(nmr, mv[:, 0:1], -1.0, rstd,
                                               ALU.mult, ALU.mult),
              reads=[("mv",), ("rstd",)], writes=[("nmr",)])
        p.add("scalar",
              lambda e: e.activation(out_ap, xa, AF.Identity, bias=nmr,
                                     scale=rstd),
              reads=[xk, ("nmr",), ("rstd",)], writes=[outk])

    def transpose128(dst_ap, dst_key, src_ap, src_key, chan):
        p.add("sync",
              lambda e: e.dma_start(dst_ap, src_ap, transpose=True),
              reads=[src_key], writes=[dst_key], dma_chan=chan, label="tr")

    def rope(ps_ap, ps_key, col0, n, out_ap, out_keys):
        s1, s2, s3 = scr[:, 0, 0:n], scr[:, 1, 0:n], scr[:, 2, 0:n]
        p.add("vector",
              lambda e: e.tensor_mul(s1, ps_ap, cos_sb[:, col0:col0 + n]),
              reads=[ps_key, ("cos",)], writes=[("rs1",)])
        p.add("vector", lambda e: e.stream_shuffle(s2, ps_ap, SHUF_MASK),
              reads=[ps_key], writes=[("rs2",)])
        p.add("vector",
              lambda e: e.tensor_mul(s3, s2, sin_sb[:, col0:col0 + n]),
              reads=[("rs2",), ("sin",)], writes=[("rs3",)])
        p.add("vector", lambda e: e.tensor_add(out_ap, s1, s3),
              reads=[("rs1",), ("rs3",)], writes=list(out_keys))

    def mm(bank_ap, bank_key, lhsT, rhs, reads, start, stop, tp=None,
           skip=False):
        extra = [] if start and not skip else [bank_key]
        return p.add(
            "tensor",
            lambda e: e.matmul(bank_ap, lhsT, rhs, start=start, stop=stop,
                               tile_position=tp, skip_group_check=skip),
            reads=list(reads) + extra, writes=[bank_key])

    def kv256(tab0, kcol0, st0, cb):
        for oc in range(KC):
            bank = pbig if oc % 2 == 0 else pbig2
            bkey = ("pbig",) if oc % 2 == 0 else ("pbig2",)
            for hc in range(HC):
                mm(bank[:, 0:256], bkey,
                   wk_sb[:, hc, 128 * oc:128 * oc + 128],
                   actT[:, hc, cb:cb + 256],
                   list(wk_pg) + [("actT", hc, cb), ("actT", hc, cb + 128)],
                   start=(hc == 0), stop=(hc == HC - 1))
            rope(bank[:, 0:256], bkey, tab0, 256,
                 kT[:, oc, kcol0:kcol0 + 256],
                 [("kT", oc, kcol0), ("kT", oc, kcol0 + 128)])
        for half in range(2):
            bank = pbig if half == 0 else pbig2
            bkey = ("pbig",) if half == 0 else ("pbig2",)
            for hc in range(HC):
                mm(bank[:, 0:KVD], bkey,
                   actT[:, hc, cb + 128 * half:cb + 128 * half + 128],
                   wv_sb[:, hc, :],
                   list(wv_pg) + [("actT", hc, cb + 128 * half)],
                   start=(hc == 0), stop=(hc == HC - 1))
            p.add("scalar",
                  lambda e, bank=bank, st=st0 + half: e.copy(
                      v_sb[:, st, :], bank[:, 0:KVD]),
                  reads=[bkey], writes=[("v", st0 + half)])

    def ln_and_transpose(xsrc, row0, dst_cb, chan_i):
        pp = (row0 // 128) % 2
        xa = x_buf[:, pp, :]
        xk = ("xbuf", pp)
        d("gpsimd", xa, xsrc[row0:row0 + 128, :], "D:x%d" % pp, writes=[xk])
        layer_norm(xk, xa, ("xn",), xn_bf[:, :])
        for hc in range(HC):
            bank = P256[hc % 2]
            bview = bank.bitcast(BF16)
            bkey = ("p256", hc % 2)
            p.add("tensor",
                  lambda e, hc=hc, bview=bview: e.transpose(
                      bview[:, 0:128], xn_bf[:, 128 * hc:128 * hc + 128],
                      id_bf[:, :]),
                  reads=[("xn",), ("id",)], writes=[bkey])
            p.add("vector",
                  lambda e, hc=hc, dst_cb=dst_cb, bview=bview: e.tensor_copy(
                      actT[:, hc, dst_cb:dst_cb + 128], bview[:, 0:128]),
                  reads=[bkey], writes=[("actT", hc, dst_cb)])

    d("gpsimd", wk_sb, wk_in[0], "D:wk", writes=list(wk_pg))
    d("gpsimd", wv_sb, wv_in[0], "D:wv", writes=list(wv_pg))
    for ot in range(NSLOT):
        cb = 256 * (ot % 2)
        for half in range(2):
            ln_and_transpose(x_oth, 256 * ot + 128 * half, cb + 128 * half,
                             2 * ot + half)
        kv256(OWN + 256 * ot, OWN + 256 * ot, OWN // 128 + 2 * ot, cb)

    NQ = int(os.environ.get("KK_NQ", cfg.NQ))
    KK_PH = int(os.environ.get("KK_PH", 9))
    for q in range(NQ):
        if q > 0:
            d("gpsimd", wk_sb, wk_in[0], "D:wk", writes=list(wk_pg))
            d("gpsimd", wv_sb, wv_in[0], "D:wv", writes=list(wv_pg))
        for g2 in range(2):
            for half in range(2):
                ln_and_transpose(x_own, 512 * q + 256 * g2 + 128 * half,
                                 256 * g2 + 128 * half, 2 * g2 + half)
            kv256(512 * q + 256 * g2, 512 * q + 256 * g2,
                  4 * q + 2 * g2, 256 * g2)

        if KK_PH < 2:
            continue
        d("gpsimd", wr_sb, wr_in[0], "D:wr", writes=list(wr_pg))
        for tt in range(4):
            bank = P256[tt % 2]
            bkey = ("p256", tt % 2)
            for hc in range(HC):
                mm(bank[:, 0:E], bkey, actT[:, hc, 128 * tt:128 * tt + 128],
                   wr_sb[:, hc, :],
                   list(wr_pg) + [("actT", hc, 128 * tt)],
                   start=(hc == 0), stop=(hc == HC - 1))
            nm = scr2[:, 48:49]
            p.add("vector",
                  lambda e, bank=bank: e.reduce_max(nm, bank[:, 0:E],
                                                    axis=AX.X, negate=True),
                  reads=[bkey], writes=[("nm",)])
            el = scr[:, 0, 0:E]
            Z = scr2[:, 49:50]
            p.add("scalar",
                  lambda e, bank=bank: e.activation(el, bank[:, 0:E], AF.Exp,
                                                    bias=nm, accum_out=Z),
                  reads=[bkey, ("nm",)], writes=[("rs1",), ("Z",)])
            mx = scr2[:, 53:61]
            p.add("vector", lambda e: e.max(out=mx, in_=el),
                  reads=[("rs1",)], writes=[("mx8",)])
            p.add("vector", lambda e: e.memset(scr2[:, 57:61], 0.0),
                  reads=[("mx8",)], writes=[("mx8",)])
            zap = scr[:, 1, 0:E]
            p.add("vector",
                  lambda e: e.match_replace(out=zap, in_to_replace=mx,
                                            in_values=el, imm_value=0.0),
                  reads=[("rs1",), ("mx8",)], writes=[("rs2",)])
            S = scr2[:, 50:51]
            p.add("vector",
                  lambda e: e.scalar_tensor_tensor(g_bf[:, :], zap, -1.0, el,
                                                   ALU.mult, ALU.add,
                                                   accum_out=S),
                  reads=[("rs1",), ("rs2",)], writes=[("gbf",), ("S",)])
            den = scr2[:, 51:52]
            p.add("vector",
                  lambda e: e.scalar_tensor_tensor(den, Z, cfg.QEPS, S,
                                                   ALU.mult, ALU.add),
                  reads=[("Z",), ("S",)], writes=[("den",)])
            rec = scr2[:, 52:53]
            p.add("vector", lambda e: e.reciprocal(rec, den),
                  reads=[("den",)], writes=[("rec",)])
            transpose128(gT_bf[:, :], ("gT",), g_bf[:, :], ("gbf",),
                         "D:tr%d" % (tt % 2))
            cbank = P256[2]
            mm(cbank[:, 0:KE], ("p256", 2), gT_bf[:, :], amat[:, :],
               [("gT",), ("amat",)], start=True, stop=True)
            p.add("vector",
                  lambda e, tt=tt, cbank=cbank: e.tensor_scalar(
                      c_sb[:, tt, :], cbank[:, 0:KE], rec, None, ALU.mult),
                  reads=[("p256", 2), ("rec",)], writes=[("c", tt)])

        if KK_PH < 3:
            continue
        nsub = (IE // 2) // CS
        steps = [(ih, sub, ke) for ih in range(2) for sub in range(nsub)
                 for ke in range(KE)]

        def gu_slab_idx(ih, sub, ke):
            return (ke * IE + ih * (IE // 2) + sub * CS) // CS

        def load_gu(si, buf):
            sl = gu_slab_idx(*steps[si])
            g_ap, g_pg = wgu[0][buf]
            u_ap, u_pg = wgu[1][buf]
            d("gpsimd", g_ap, wg_in[sl], "D:wg%d" % buf, writes=list(g_pg))
            d("gpsimd", u_ap, wu_in[sl], "D:wu%d" % buf, writes=list(u_pg))

        load_gu(0, 0)
        for si, (ih, sub, ke) in enumerate(steps):
            buf = si % 2
            if si + 1 < len(steps):
                load_gu(si + 1, (si + 1) % 2)
            g_ap, gpg = wgu[0][buf]
            u_ap, upg = wgu[1][buf]
            for tt in range(4):
                pg = P256[(si * 4 + tt) % 2]
                pgk = ("p256", (si * 4 + tt) % 2)
                pu = P256[2 + (si * 4 + tt) % 2]
                puk = ("p256", 2 + (si * 4 + tt) % 2)
                for hc in range(HC):
                    mm(pg[:, :], pgk, actT[:, hc, 128 * tt:128 * tt + 128],
                       g_ap[:, hc, :],
                       list(gpg) + [("actT", hc, 128 * tt)],
                       start=(hc == 0), stop=(hc == HC - 1))
                for hc in range(HC):
                    mm(pu[:, :], puk, actT[:, hc, 128 * tt:128 * tt + 128],
                       u_ap[:, hc, :],
                       list(upg) + [("actT", hc, 128 * tt)],
                       start=(hc == 0), stop=(hc == HC - 1))
                pp2 = tt % 2
                p.add("scalar",
                      lambda e, pg=pg, pp2=pp2: e.activation(
                          sg_bf[:, pp2, :], pg[:, :], AF.Silu),
                      reads=[pgk], writes=[("sg", pp2)])
                p.add("vector",
                      lambda e, pu=pu, pp2=pp2: e.tensor_copy(
                          su_bf[:, pp2, :], pu[:, :]),
                      reads=[puk], writes=[("su", pp2)])
                p.add("vector",
                      lambda e, pp2=pp2: e.tensor_mul(
                          bas_bf[:, pp2, :], sg_bf[:, pp2, :],
                          su_bf[:, pp2, :]),
                      reads=[("sg", pp2), ("su", pp2)], writes=[("bas", pp2)])
                yk = ("ya", tt)
                if ke == 0:
                    p.add("vector",
                          lambda e, pp2=pp2, tt=tt, ke=ke: e.tensor_scalar(
                              y_acc[:, tt, :], bas_bf[:, pp2, :],
                              c_sb[:, tt, ke:ke + 1], None, ALU.mult),
                          reads=[("bas", pp2), ("c", tt)], writes=[yk])
                elif ke < KE - 1:
                    p.add("vector",
                          lambda e, pp2=pp2, tt=tt, ke=ke:
                          e.scalar_tensor_tensor(
                              y_acc[:, tt, :], bas_bf[:, pp2, :],
                              c_sb[:, tt, ke:ke + 1],
                              y_acc[:, tt, :], ALU.mult, ALU.add),
                          reads=[("bas", pp2), ("c", tt), yk], writes=[yk])
                else:
                    p.add("vector",
                          lambda e, pp2=pp2, tt=tt, ke=ke:
                          e.scalar_tensor_tensor(
                              ybf_s[:, tt, :], bas_bf[:, pp2, :],
                              c_sb[:, tt, ke:ke + 1],
                              y_acc[:, tt, :], ALU.mult, ALU.add),
                          reads=[("bas", pp2), ("c", tt), yk],
                          writes=[("ybs", tt)])
            if ke == KE - 1:
                for tt in range(4):
                    for ib in range(CS // 128):
                        icol = ih * (IE // 2) + sub * CS + 128 * ib
                        ic = icol // 128
                        transpose128(
                            yT[:, ic, 128 * tt:128 * tt + 128],
                            ("yT", ic, 128 * tt),
                            ybf_s[:, tt, 128 * ib:128 * ib + 128],
                            ("ybs", tt), "D:tr%d" % (tt % 2))

        if KK_PH < 4:
            continue
        nsh = cfg.SHC // CS

        def load_sh(si, buf):
            g_ap, g_pg = wgu[0][buf]
            u_ap, u_pg = wgu[1][buf]
            d("gpsimd", g_ap, swg_in[si], "D:wg%d" % buf, writes=list(g_pg))
            d("gpsimd", u_ap, swu_in[si], "D:wu%d" % buf, writes=list(u_pg))

        load_sh(0, 0)
        for si in range(nsh):
            buf = si % 2
            if si + 1 < nsh:
                load_sh(si + 1, (si + 1) % 2)
            g_ap, gpg = wgu[0][buf]
            u_ap, upg = wgu[1][buf]
            for tt in range(4):
                pg = P256[(si * 4 + tt) % 2]
                pgk = ("p256", (si * 4 + tt) % 2)
                pu = P256[2 + (si * 4 + tt) % 2]
                puk = ("p256", 2 + (si * 4 + tt) % 2)
                for hc in range(HC):
                    mm(pg[:, :], pgk, actT[:, hc, 128 * tt:128 * tt + 128],
                       g_ap[:, hc, :],
                       list(gpg) + [("actT", hc, 128 * tt)],
                       start=(hc == 0), stop=(hc == HC - 1))
                for hc in range(HC):
                    mm(pu[:, :], puk, actT[:, hc, 128 * tt:128 * tt + 128],
                       u_ap[:, hc, :],
                       list(upg) + [("actT", hc, 128 * tt)],
                       start=(hc == 0), stop=(hc == HC - 1))
                pp2 = tt % 2
                p.add("scalar",
                      lambda e, pg=pg, pp2=pp2: e.activation(
                          sg_bf[:, pp2, :], pg[:, :], AF.Silu),
                      reads=[pgk], writes=[("sg", pp2)])
                p.add("vector",
                      lambda e, pu=pu, pp2=pp2: e.tensor_copy(
                          su_bf[:, pp2, :], pu[:, :]),
                      reads=[puk], writes=[("su", pp2)])
                p.add("vector",
                      lambda e, pp2=pp2: e.tensor_mul(
                          sh_bf[:, pp2, :], sg_bf[:, pp2, :], su_bf[:, pp2, :]),
                      reads=[("sg", pp2), ("su", pp2)], writes=[("bas", pp2)])
                for ib in range(CS // 128):
                    ic = (si * CS + 128 * ib) // 128
                    transpose128(shT[:, ic, 128 * tt:128 * tt + 128],
                                 ("shT", ic, 128 * tt),
                                 sh_bf[:, pp2, 128 * ib:128 * ib + 128],
                                 ("bas", pp2), "D:tr%d" % (tt % 2))

        if KK_PH < 5:
            continue
        def load_wq(si, buf):
            q_ap, q_pg = wq_b[buf]
            d("gpsimd", q_ap, wq_in[si], "D:wq%d" % buf, writes=list(q_pg))

        load_wq(0, 0)
        for si in range(HC // 2):
            buf = si % 2
            if si + 1 < HC // 2:
                load_wq(si + 1, (si + 1) % 2)
            q_ap, q_pg = wq_b[buf]
            for oc2 in range(2):
                oc = 2 * si + oc2
                bank = pbig if oc % 2 == 0 else pbig2
                bkey = ("pbig",) if oc % 2 == 0 else ("pbig2",)
                for hc in range(HC):
                    mm(bank[:, 0:512], bkey,
                       q_ap[:, hc, 128 * oc2:128 * oc2 + 128],
                       actT[:, hc, :],
                       list(q_pg) + [("actT", hc, o)
                                     for o in (0, 128, 256, 384)],
                       start=(hc == 0), stop=(hc == HC - 1))
                for hq in range(2):
                    rope(bank[:, 256 * hq:256 * hq + 256], bkey,
                         512 * q + 256 * hq, 256,
                         qT_v[:, oc, 256 * hq:256 * hq + 256],
                         [("qT", oc, 256 * hq)] + list(qT_pg))

        if KK_PH < 6:
            continue
        KK_NC = int(os.environ.get("KK_NC", 99))
        KK_TR = int(os.environ.get("KK_TR", 99))
        KK_NORM = int(os.environ.get("KK_NORM", 1))
        KK_EXP = int(os.environ.get("KK_EXP", 1))
        KK_PV = int(os.environ.get("KK_PV", 1))
        KK_SUM = int(os.environ.get("KK_SUM", 1))
        for sl in range(2):
            j = 2 * q + sl
            qc0 = 256 * sl
            ntr = min(cfg.trips(j), KK_TR)
            def emit_zq(c, hf):
                for ii in range(2):
                    qch = 4 * c + 2 * hf + ii
                    p.add("vector",
                          lambda e, qch=qch, ii=ii, qc0=qc0: e.tensor_copy(
                              zq[0:64, 2 * ii, :],
                              qT_v[0:64, qch, qc0:qc0 + 256]),
                          reads=[("qT", qch, qc0), qT_pg[0]],
                          writes=[("zq", 2 * ii)])
                    p.add("vector",
                          lambda e, qch=qch, ii=ii, qc0=qc0: e.tensor_copy(
                              zq[64:128, 2 * ii + 1, :],
                              qT_v[64:128, qch, qc0:qc0 + 256]),
                          reads=[("qT", qch, qc0), qT_pg[0]],
                          writes=[("zq", 2 * ii + 1)])

            def emit_scores(j, sc, cc):
                if sc < 2 * j + 2:
                    kcol = 128 * sc
                else:
                    kcol = OWN + 128 * (sc - (2 * j + 2))
                diag = (sc == 2 * j) or (sc == 2 * j + 1)
                sreg = pbig if sc % 2 == 0 else pbig2
                srk = ("pbig",) if sc % 2 == 0 else ("pbig2",)
                for ii in range(2):
                    mm(sreg[:, 512 * ii:512 * ii + 256], srk,
                       kT[:, cc, kcol:kcol + 128], zq[:, 2 * ii, :],
                       [("kT", cc, kcol), ("zq", 2 * ii)],
                       start=True, stop=False)
                    mm(sreg[:, 512 * ii + 256:512 * ii + 512], srk,
                       kT[:, cc, kcol:kcol + 128], zq[:, 2 * ii + 1, :],
                       [("kT", cc, kcol), ("zq", 2 * ii + 1)],
                       start=False, stop=True)
                pp3 = sc % 3
                bcol = bias_sb[:, cfg.bias_col(j, sc):cfg.bias_col(j, sc) + 1]
                if not KK_EXP:
                    return
                if not diag:
                    p.add("scalar",
                          lambda e, sreg=sreg, pp3=pp3, bcol=bcol:
                          e.activation(exp_bf[:, pp3, :], sreg[:, :],
                                       AF.Exp, bias=bcol, scale=0.125),
                          reads=[srk, ("bias",)], writes=[("exp", pp3)])
                else:
                    dd = sc - 2 * j
                    if dd == 0:
                        exd, exdk = expd_bf, [("rs1",), ("rs2",)]
                    else:
                        exd = y_acc[:, 0:2, :].rearrange(
                            "p a b -> p (a b)").bitcast(BF16)
                        exdk = [("ya", 0), ("ya", 1)]
                    p.add("scalar",
                          lambda e, sreg=sreg, bcol=bcol, exd=exd:
                          e.activation(exd, sreg[:, :],
                                       AF.Exp, bias=bcol, scale=0.125),
                          reads=[srk, ("bias",)], writes=exdk)
                    p.add("vector",
                          lambda e, pp3=pp3, dd=dd, exd=exd: e.tensor_mul(
                              exp_bf[:, pp3, :].rearrange(
                                  "p (a b) -> p a b", a=4),
                              exd.rearrange("p (a b) -> p a b", a=4),
                              tri_sb[:, dd:dd + 1, :].to_broadcast(
                                  [128, 4, 256])),
                          reads=exdk + [("tri",)],
                          writes=[("exp", pp3)])

            def emit_pv(sc, ntr, st):
                if not (KK_EXP and KK_PV):
                    return
                pp3 = sc % 3
                ek = ("exp", pp3)
                first = (sc == 0)
                last = (sc == ntr - 1)
                for ii in range(2):
                    ex_lo = exp_bf[:, pp3, 512 * ii:512 * ii + 256]
                    ex_hi = exp_bf[:, pp3, 512 * ii + 256:512 * ii + 512]
                    mm(P256[ii][0:64, :], ("p256", ii),
                       v_sb[:, st, 64 * 2 * c:64 * 2 * c + 64],
                       ex_lo, [("v", st), ek],
                       start=first, stop=last, tp=(0, 0))
                    mm(P256[ii][64:128, :], ("p256", ii),
                       v_sb[:, st, 64 * (2 * c + 1):64 * (2 * c + 1) + 64],
                       ex_hi, [("v", st), ek],
                       start=first, stop=last, tp=(0, 64), skip=True)
                if KK_SUM:
                    for ii in range(2):
                        ex_lo = exp_bf[:, pp3, 512 * ii:512 * ii + 256]
                        ex_hi = exp_bf[:, pp3, 512 * ii + 256:512 * ii + 512]
                        mm(P256[2 + ii][0:64, :], ("p256", 2 + ii),
                           ones_bf[:, :], ex_lo, [("ones",), ek],
                           start=first, stop=last, tp=(0, 0))
                        mm(P256[2 + ii][64:128, :], ("p256", 2 + ii),
                           ones_bf[:, :], ex_hi, [("ones",), ek],
                           start=first, stop=last, tp=(0, 64), skip=True)

            passes = [(c, hf) for c in range(min(KC, KK_NC)) for hf in range(2)]
            emit_zq(*passes[0])
            pre = 0
            for pi, (c, hf) in enumerate(passes):
                pend = []
                for sc in range(ntr):
                    st = sc if sc < 2 * j + 2 else OWN // 128 + (sc - (2 * j + 2))
                    if sc >= pre:
                        emit_scores(j, sc, c)
                    while len(pend) >= 2:
                        args = pend.pop(0)
                        emit_pv(*args)
                    pend.append((sc, ntr, st))
                for args in pend:
                    emit_pv(*args)
                pend = []
                if pi + 1 < len(passes):
                    emit_zq(*passes[pi + 1])
                    pre = min(2, ntr)
                    for sc2 in range(pre):
                        emit_scores(j, sc2, passes[pi + 1][0])
                else:
                    pre = 0
                for ii in range(2 if KK_NORM else 0):
                    i = 2 * hf + ii
                    qch = 4 * c + i
                    pv = P256[ii]
                    pvk = ("p256", ii)
                    sm = P256[2 + ii]
                    smk = ("p256", 2 + ii)
                    rcp = scr[:, 2, :]
                    p.add("vector",
                          lambda e, sm=sm, rcp=rcp: e.reciprocal(
                              rcp, sm[:, :]),
                          reads=[smk], writes=[("rs3",)])
                    p.add("vector",
                          lambda e, pv=pv, qch=qch, qc0=qc0, rcp=rcp:
                          e.tensor_mul(
                              actT[0:64, qch, qc0:qc0 + 256],
                              pv[0:64, :], rcp[0:64, :]),
                          reads=[pvk, ("rs3",)],
                          writes=[("actT", qch, qc0),
                                  ("actT", qch, qc0 + 128)])
                    p.add("vector",
                          lambda e, pv=pv, qch=qch, qc0=qc0, rcp=rcp:
                          e.tensor_mul(
                              actT[64:128, qch, qc0:qc0 + 256],
                              pv[64:128, :], rcp[64:128, :]),
                          reads=[pvk, ("rs3",)],
                          writes=[("actT", qch, qc0),
                                  ("actT", qch, qc0 + 128)])

        def load_b6(osi, buf):
            (o_ap, o_pg), (d_ap, d_pg), (s_ap, s_pg) = b6b[buf]
            d("gpsimd", o_ap, wo_in[osi], "D:b6o%d" % buf, writes=list(o_pg))
            d("gpsimd", d_ap, wd_in[osi], "D:b6d%d" % buf, writes=list(d_pg))
            d("gpsimd", s_ap, swd_in[osi], "D:b6s%d" % buf, writes=list(s_pg))

        nos = H // OS
        load_b6(0, 0)
        for osi in range(nos):
            buf = osi % 2
            if osi + 1 < nos:
                load_b6(osi + 1, (osi + 1) % 2)
            (o_ap, o_pg), (d_ap, d_pg), (s_ap, s_pg) = b6b[buf]
            for tt in range(4):
                bank = P256[(osi * 4 + tt) % 4]
                bkey = ("p256", (osi * 4 + tt) % 4)
                nmm = HC + NIC + NSC
                k_ = 0
                for hc in range(HC):
                    mm(bank[:, :], bkey, actT[:, hc, 128 * tt:128 * tt + 128],
                       o_ap[:, hc, :],
                       list(o_pg) + [("actT", hc, 128 * tt)],
                       start=(k_ == 0), stop=False)
                    k_ += 1
                for ic in range(NIC):
                    mm(bank[:, :], bkey, yT[:, ic, 128 * tt:128 * tt + 128],
                       d_ap[:, ic, :],
                       list(d_pg) + [("yT", ic, 128 * tt)],
                       start=False, stop=False)
                    k_ += 1
                for ic in range(NSC):
                    mm(bank[:, :], bkey, shT[:, ic, 128 * tt:128 * tt + 128],
                       s_ap[:, ic, :],
                       list(s_pg) + [("shT", ic, 128 * tt)],
                       start=False, stop=(ic == NSC - 1))
                    k_ += 1
                ppx = tt % 2
                xsl = xslice[:, ppx, :]
                xslk = ("xs", ppx)
                row0 = 512 * q + 128 * tt
                d("sync", xsl, x_own[row0:row0 + 128, OS * osi:OS * osi + OS],
                  "D:xs%d" % ppx, writes=[xslk])
                o_st = ost[:, ppx, :]
                ok = ("ost", ppx)
                p.add("vector",
                      lambda e, bank=bank, xsl=xsl, o_st=o_st: e.tensor_add(
                          o_st, bank[:, :], xsl),
                      reads=[bkey, xslk], writes=[ok])
                d("sync", out_d[row0:row0 + 128, OS * osi:OS * osi + OS],
                  o_st, "D:out", reads=[ok])

    with nc.allow_low_precision(reason="bf16 matmul operand staging"):
        with nc.Block() as block:
            p.emit(block)
    for t in reversed(ctx):
        t.__exit__(None, None, None)
    return nc, p


def kernel(**inputs):
    cfg = Cfg(tiny=False)
    shared, per_core = host_prep(cfg, inputs)
    nc, _ = build_nc(cfg)
    in_maps = []
    for core in range(cfg.NCORE):
        m = {"x_own": per_core[core]["x_own"],
             "x_oth": per_core[core]["x_oth"],
             "rcos": per_core[core]["rcos"],
             "rsin": per_core[core]["rsin"],
             "biasm": per_core[core]["biasm"],
             "tri": shared["tri"], "amps": shared["amps"],
             "escale": shared["escale"], "ident": shared["ident"],
             "wq": shared["wq"],
             "wk": shared["wk"], "wv": shared["wv"], "wo": shared["wo"],
             "wr": shared["wr"], "wg": shared["wg"], "wu": shared["wu"],
             "wd": shared["wd"], "swg": shared["swg"], "swu": shared["swu"],
             "swd": shared["swd"]}
        in_maps.append(m)
    res = run_bass_kernel_spmd(nc, in_maps, core_ids=list(range(cfg.NCORE)))
    out = np.zeros((cfg.B, cfg.T, cfg.H), np.float32)
    for core in range(cfg.NCORE):
        seq, par = core // 2, core % 2
        o = res.results[core]["out"].reshape(cfg.NSLOT, cfg.SLOT, cfg.H)
        for j in range(cfg.NSLOT):
            t = 2 * j + par
            out[seq, cfg.SLOT * t:cfg.SLOT * (t + 1), :] = o[j]
    return out


# version 17
# speedup vs baseline: 1.0787x; 1.0495x over previous
import os
import sys
sys.path.insert(0, "/opt/trn_rl_repo")
import numpy as np
import ml_dtypes
import concourse.bass as bass
import concourse.mybir as mybir
from concourse.bass_utils import run_bass_kernel_spmd

F32 = mybir.dt.float32
BF16 = mybir.dt.bfloat16
AF = mybir.ActivationFunctionType
ALU = mybir.AluOpType
AX = mybir.AxisListType

ENGS = ("tensor", "vector", "scalar", "gpsimd", "sync")
SHUF_MASK = []
for _i in range(16):
    SHUF_MASK += [2 * _i + 1, 2 * _i]


class Cfg:
    def __init__(self, tiny=False):
        if tiny:
            self.B, self.T, self.H = 1, 1024, 512
            self.NH, self.NKV = 8, 2
            self.E, self.KE, self.IE = 128, 8, 512
            self.ISH = 256
        else:
            self.B, self.T, self.H = 4, 4096, 2048
            self.NH, self.NKV = 32, 8
            self.E, self.KE, self.IE = 128, 8, 1024
            self.ISH = 1024
        self.DH = 64
        self.HC = self.H // 128
        self.SLOT = 256
        self.NSLOT = self.T // (2 * self.SLOT)
        self.NQ = self.NSLOT // 2
        self.NCORE = 2 * self.B
        self.KV = self.T
        self.NKT = self.T // 128
        self.OWN = self.NSLOT * self.SLOT
        self.KC = self.NKV * self.DH // 128
        self.GCOL = self.KE * self.IE
        self.SHC = 2 * self.ISH
        self.CS = 256
        self.OS = 256
        self.QEPS = 1e-8
        self.LNEPS = 1e-5
        self.EXPB = -30.0
        self.NEGB = -330.0

    def trips(self, j):
        return 4 * j + 4

    def bias_col(self, j, sc):
        return sum(self.trips(jj) for jj in range(j)) + sc

    @property
    def nbias(self):
        return sum(self.trips(j) for j in range(self.NSLOT))


class Op:
    __slots__ = ("eng", "fn", "idx", "waits", "inc", "is_dma", "label")

    def __init__(self, eng, fn, idx, is_dma, label):
        self.eng, self.fn, self.idx = eng, fn, idx
        self.waits = []
        self.inc = None
        self.is_dma = is_dma
        self.label = label


class PB:
    def __init__(self, nc):
        self.nc = nc
        self.ops = {e: [] for e in ENGS}
        self.sems = {}
        self.eng_inc_count = {e: 0 for e in ENGS}
        self.eng_incs = {e: [] for e in ENGS}
        self.chan_total = {}
        self.last_writer = {}
        self.readers = {}
        self.clock = {e: {} for e in ENGS}
        self.n_waits = 0

    def sem(self, name):
        if name not in self.sems:
            self.sems[name] = self.nc.alloc_semaphore(name.replace(":", "_"))
        return self.sems[name]

    def _resolve_eng_dep(self, eng, idx):
        incs = self.eng_incs[eng]
        lo, hi = 0, len(incs)
        while lo < hi:
            mid = (lo + hi) // 2
            if incs[mid][0] >= idx:
                hi = mid
            else:
                lo = mid + 1
        if lo < len(incs):
            return ("E:" + eng, incs[lo][1])
        op = self.ops[eng][idx]
        assert op.inc is None
        self.eng_inc_count[eng] += 1
        val = self.eng_inc_count[eng]
        op.inc = ("E:" + eng, val)
        incs.append((idx, val))
        return ("E:" + eng, val)

    def _need(self, me_eng, waits, dep):
        deng, didx = dep
        op = self.ops[deng][didx]
        if op.is_dma:
            sem = op.inc[0]
            val = self.chan_total[sem]
        else:
            sem, val = self._resolve_eng_dep(deng, didx)
        myc = self.clock[me_eng]
        if myc.get(sem, 0) < val:
            waits[sem] = max(waits.get(sem, 0), val)
            myc[sem] = val

    def add(self, eng, fn, reads=(), writes=(), dma_chan=None, label=""):
        idx = len(self.ops[eng])
        is_dma = dma_chan is not None
        op = Op(eng, fn, idx, is_dma, label)
        self.ops[eng].append(op)
        waits = {}
        deps = []
        for k in reads:
            w = self.last_writer.get(k)
            if w is not None:
                deps.append(w)
        for k in writes:
            w = self.last_writer.get(k)
            if w is not None:
                deps.append(w)
            deps.extend(self.readers.get(k, ()))
        me = (eng, idx)
        for dep in deps:
            if dep == me:
                continue
            if eng == "tensor" and dep[0] == "tensor":
                continue
            self._need(eng, waits, dep)
        op.waits = sorted(waits.items())
        self.n_waits += len(op.waits)
        if is_dma:
            tot = self.chan_total.get(dma_chan, 0) + 16
            self.chan_total[dma_chan] = tot
            op.inc = (dma_chan, tot)
        for k in writes:
            self.last_writer[k] = me
            self.readers[k] = []
        for k in reads:
            self.readers.setdefault(k, []).append(me)
        return me

    def emit(self, block, final_eng="sync"):
        for e in ENGS:
            if self.eng_inc_count[e]:
                self.sem("E:" + e)
        for chan in sorted(self.chan_total):
            self.sem(chan)
        engines = {"tensor": block.tensor, "vector": block.vector,
                   "scalar": block.scalar, "gpsimd": block.gpsimd,
                   "sync": block.sync}

        def mk(eng_name):
            def section(eng):
                for op in self.ops[eng_name]:
                    for sname, val in op.waits:
                        eng.wait_ge(self.sems[sname], val)
                    inst = op.fn(eng)
                    if op.inc is not None:
                        inst.then_inc(self.sems[op.inc[0]],
                                      16 if op.is_dma else 1)
                if eng_name == final_eng:
                    for chan, tot in sorted(self.chan_total.items()):
                        eng.wait_ge(self.sems[chan], tot)
                    for e in ENGS:
                        if e != eng_name and self.eng_inc_count[e]:
                            eng.wait_ge(self.sems["E:" + e], self.eng_inc_count[e])
            return section

        for name, dec in engines.items():
            dec(mk(name))


def host_prep(cfg, inputs):
    f32 = np.float32
    H = cfg.H
    Wq = np.asarray(inputs["L0_attn_Wq"], f32)
    Wk = np.asarray(inputs["L0_attn_Wk"], f32)
    Wv = np.asarray(inputs["L0_attn_Wv"], f32)
    Wo = np.asarray(inputs["L0_attn_Wo"], f32)
    w_router = np.asarray(inputs["L0_ffn_w_router"], f32)
    w_gate = np.asarray(inputs["L0_ffn_w_gate"], f32)
    w_up = np.asarray(inputs["L0_ffn_w_up"], f32)
    w_down = np.asarray(inputs["L0_ffn_w_down"], f32)
    amps = np.asarray(inputs["L0_ffn_amplitudes"], f32)
    escale = np.asarray(inputs["L0_ffn_expert_scale"], f32)
    swg = np.asarray(inputs["L0_ffn_shared_w_gate"], f32)
    swu = np.asarray(inputs["L0_ffn_shared_w_up"], f32)
    swd = np.asarray(inputs["L0_ffn_shared_w_down"], f32)
    x = np.asarray(inputs["L0_attn_x"], f32)

    qperm = []
    for c in range(cfg.NKV // 2):
        for i in range(4):
            for h in (8 * c + i, 8 * c + 4 + i):
                qperm.extend(range(64 * h, 64 * h + 64))
    qperm = np.array(qperm)

    def slab(w, cs):
        K, C = w.shape
        return np.ascontiguousarray(
            w.reshape(K // 128, 128, C // cs, cs).transpose(2, 1, 0, 3))

    shared = {
        "wq": slab(np.ascontiguousarray(Wq.T)[:, qperm], 256),
        "wk": slab(np.ascontiguousarray(Wk.T), Wk.shape[0]),
        "wv": slab(np.ascontiguousarray(Wv.T), Wv.shape[0]),
        "wo": slab(np.ascontiguousarray(Wo.T)[qperm, :], cfg.OS),
        "wr": slab(w_router, cfg.E),
        "wg": slab(w_gate, cfg.CS),
        "wu": slab(w_up, cfg.CS),
        "wd": slab(w_down, cfg.OS),
        "swg": slab(swg, cfg.CS),
        "swu": slab(swu, cfg.CS),
        "swd": slab(swd, cfg.OS),
        "amps": np.ascontiguousarray(amps.reshape(cfg.E, 2 * cfg.KE)),
        "escale": np.ascontiguousarray(escale.reshape(cfg.E, 1)),
    }
    shared_ident = np.eye(128, dtype=f32)
    tri = np.zeros((128, 2, cfg.SLOT), f32)
    for dd in range(2):
        tri[:, dd, :] = (128 * dd + np.arange(128)[:, None]
                         <= np.arange(cfg.SLOT)[None, :])
    shared["tri"] = tri
    shared["ident"] = shared_ident

    invf = 1.0 / 10000.0 ** (np.arange(0, cfg.DH, 2, dtype=f32) / cfg.DH)
    didx = np.arange(cfg.DH)
    ang_i = didx % (cfg.DH // 2)
    sign = np.where(didx % 2 == 0, -1.0, 1.0).astype(f32)

    per_core = []
    for core in range(cfg.NCORE):
        seq, par = core // 2, core % 2
        t_own = [2 * j + par for j in range(cfg.NSLOT)]
        t_oth = [2 * j + (1 - par) for j in range(cfg.NSLOT)]
        xs = x[seq].reshape(2 * cfg.NSLOT, cfg.SLOT, H)
        x_own = np.ascontiguousarray(xs[t_own].reshape(cfg.OWN, H))
        x_oth = np.ascontiguousarray(xs[t_oth].reshape(cfg.OWN, H))
        pos = np.concatenate(
            [cfg.SLOT * t + np.arange(cfg.SLOT) for t in t_own] +
            [cfg.SLOT * t + np.arange(cfg.SLOT) for t in t_oth]).astype(f32)
        ang = pos[None, :] * invf[ang_i][:, None]
        ctab = np.ascontiguousarray(
            np.tile(np.cos(ang), (2, 1)).astype(ml_dtypes.bfloat16))
        stab = np.ascontiguousarray(
            np.tile(np.sin(ang) * sign[:, None],
                    (2, 1)).astype(ml_dtypes.bfloat16))
        biasm = np.full((128, cfg.nbias), cfg.EXPB, f32)
        for j in range(cfg.NSLOT):
            q0 = cfg.SLOT * t_own[j]
            for oc in range(2 * j + 2):
                col = cfg.bias_col(j, 2 * j + 2 + oc)
                kpos = (cfg.SLOT * t_oth[oc // 2] + 128 * (oc % 2)
                        + np.arange(128))
                biasm[:, col] = np.where(kpos < q0, cfg.EXPB, cfg.NEGB)
        per_core.append({"x_own": x_own, "x_oth": x_oth, "rcos": ctab,
                         "rsin": stab, "biasm": biasm})
    return shared, per_core


def build_nc(cfg):
    nc = bass.Bass("TRN2", target_bir_lowering=False)
    H, HC, KC = cfg.H, cfg.HC, cfg.KC
    SLOT, NSLOT, OWN, KV = cfg.SLOT, cfg.NSLOT, cfg.OWN, cfg.KV
    CS, OS, KE, IE, E = cfg.CS, cfg.OS, cfg.KE, cfg.IE, cfg.E
    KVD = cfg.NKV * 64
    NIC = IE // 128
    NSC = cfg.SHC // 128

    def din(n, s, k="ExternalInput"):
        return nc.dram_tensor(n, s, F32, kind=k).ap()

    x_own = din("x_own", [OWN, H])
    x_oth = din("x_oth", [OWN, H])
    rcos = nc.dram_tensor("rcos", [128, 2 * OWN], BF16, kind="ExternalInput").ap()
    rsin = nc.dram_tensor("rsin", [128, 2 * OWN], BF16, kind="ExternalInput").ap()
    biasm = din("biasm", [128, cfg.nbias])
    tri_in = din("tri", [128, 2, SLOT])
    amps_in = din("amps", [E, 2 * KE])
    id_in = din("ident", [128, 128])
    esc_in = din("escale", [E, 1])
    wq_in = din("wq", [H // 256, 128, HC, 256])
    wk_in = din("wk", [1, 128, HC, KVD])
    wv_in = din("wv", [1, 128, HC, KVD])
    wo_in = din("wo", [H // OS, 128, HC, OS])
    wr_in = din("wr", [1, 128, HC, E])
    wg_in = din("wg", [cfg.GCOL // CS, 128, HC, CS])
    wu_in = din("wu", [cfg.GCOL // CS, 128, HC, CS])
    wd_in = din("wd", [H // OS, 128, NIC, OS])
    swg_in = din("swg", [cfg.SHC // CS, 128, HC, CS])
    swu_in = din("swu", [cfg.SHC // CS, 128, HC, CS])
    swd_in = din("swd", [H // OS, 128, NSC, OS])
    out_d = din("out", [OWN, H], k="ExternalOutput")

    ctx = []

    def sb(name, shape, dtype, **kw):
        t = nc.sbuf_tensor(name, shape, dtype, **kw)
        ctx.append(t)
        return t.__enter__()

    def psum(name, shape):
        t = nc.psum_tensor(name, shape, F32)
        ctx.append(t)
        return t.__enter__()

    kT = sb("kT", [128, KC, KV], BF16)
    v_sb = sb("v_sb", [128, cfg.NKT, KVD], BF16)
    cos_sb = sb("cos_sb", [128, 2 * OWN], BF16)
    sin_sb = sb("sin_sb", [128, 2 * OWN], BF16)
    bias_sb = sb("bias_sb", [128, cfg.nbias], F32)
    tri_sb = sb("tri_sb", [128, 2, SLOT], BF16)
    ones_bf = sb("ones_bf", [128, 64], BF16)
    id_bf = sb("id_bf", [128, 128], BF16)
    amat = sb("amat", [128, KE], BF16)
    x_buf = sb("x_buf", [128, 2, H], F32)
    xslice = sb("xslice", [128, 2, OS], F32)
    xn_bf = sb("xn_bf", [128, H], BF16)
    actT = sb("actT", [128, HC, 512], BF16, is_transpose_dest=True)
    shT = sb("shT", [128, NSC, 512], BF16, is_transpose_dest=True)
    yT = sb("yT", [128, NIC, 512], BF16, is_transpose_dest=True)
    y_acc = sb("y_acc", [128, 4, 256], F32)
    ybf_s = sb("ybf_s", [128, 4, 256], BF16)
    sg_bf = sb("sg_bf", [128, 2, CS], BF16)
    su_bf = sb("su_bf", [128, 2, CS], BF16)
    bas_bf = sb("bas_bf", [128, 2, CS], BF16)
    sh_bf = bas_bf
    exp_bf = sb("exp_bf", [128, 3, 1024], BF16)
    g_bf = sb("g_bf", [128, 128], BF16)
    gT_bf = sb("gT_bf", [128, 128], BF16, is_transpose_dest=True)
    c_sb = sb("c_sb", [128, 4, KE], F32)
    zq = sb("zq", [128, 4, 256], BF16)
    scr = sb("scr", [128, 3, 256], F32)
    expd_bf = scr[:, 0:2, :].rearrange("p a b -> p (a b)").bitcast(BF16)
    scr2 = sb("scr2", [128, 64], F32)
    ln6 = sb("ln6", [128, 24], F32)
    ost = sb("ost", [128, 2, OS], F32)
    arena = sb("arena", [128, 20480], BF16)

    def ar(off_kb, shape):
        n = int(np.prod(shape))
        off = off_kb * 512
        v = arena[:, off:off + n]
        if len(shape) == 2:
            v = v.rearrange("p (a b) -> p a b", a=shape[0], b=shape[1])
        size_kb = (2 * n + 1023) // 1024
        pages = tuple(("ar", i) for i in
                      range(off_kb // 4, (off_kb + size_kb + 3) // 4))
        return v, pages

    kb_wkv = (HC * KVD * 2 + 1023) // 1024
    wk_sb, wk_pg = ar(0, (HC, KVD))
    wv_sb, wv_pg = ar(16, (HC, KVD))
    wr_sb, wr_pg = ar(32, (HC, E))
    kb_gu = (HC * CS * 2 + 1023) // 1024
    wgu = []
    for w in range(2):
        wgu.append([ar(kb_gu * b + 2 * kb_gu * w, (HC, CS)) for b in range(2)])
    wq_b = [ar(kb_gu * b, (HC, 256)) for b in range(2)]
    qT_v, qT_pg = ar(24, (HC, 512))
    kb_o = (HC * OS * 2 + 1023) // 1024
    kb_d = (NIC * OS * 2 + 1023) // 1024
    kb_s = (NSC * OS * 2 + 1023) // 1024
    kb_b6 = kb_o + kb_d + kb_s
    b6b = []
    for b in range(2):
        base = kb_b6 * b
        o, opg = ar(base, (HC, OS))
        d_, dpg = ar(base + kb_o, (NIC, OS))
        s_, spg = ar(base + kb_o + kb_d, (NSC, OS))
        b6b.append(((o, opg), (d_, dpg), (s_, spg)))

    pbig = psum("pbig", [128, 1024])
    pbig2 = psum("pbig2", [128, 1024])
    P256 = [psum("p256_%d" % i, [128, 256]) for i in range(4)]

    p = PB(nc)

    def d(eng, dst, src, chan, reads=(), writes=(), label=""):
        return p.add(eng, lambda e, dst=dst, src=src: e.dma_start(dst, src),
                     reads=reads, writes=writes, dma_chan=chan, label=label)

    amp_sb = scr2[:, 24:24 + 2 * KE]
    esc_sb = scr2[:, 40:41]
    d("sync", cos_sb[:, :], rcos, "D:pre", writes=[("cos",)])
    d("sync", sin_sb[:, :], rsin, "D:pre", writes=[("sin",)])
    d("sync", bias_sb[:, :], biasm, "D:pre", writes=[("bias",)])
    d("gpsimd", tri_sb[:, :, :], tri_in, "D:preg", writes=[("tri",)])
    d("gpsimd", id_bf[:, :], id_in, "D:preg", writes=[("id",)])
    d("sync", amp_sb, amps_in, "D:pre", writes=[("amp",)])
    d("sync", esc_sb, esc_in, "D:pre", writes=[("esc",)])
    p.add("vector", lambda e: e.memset(ones_bf[:, :], 1.0), writes=[("ones",)])
    p.add("vector", lambda e: e.memset(zq[:, :, :], 0.0),
          writes=[("zq", v) for v in range(4)])

    a0 = amp_sb.rearrange("p (k two) -> p k two", two=2)[:, :, 0]
    a1 = amp_sb.rearrange("p (k two) -> p k two", two=2)[:, :, 1]
    t0 = scr2[:, 0:KE]
    t1 = scr2[:, 8:16]
    p.add("vector", lambda e: e.tensor_mul(t0, a0, a0),
          reads=[("amp",)], writes=[("t0",)])
    p.add("vector", lambda e: e.tensor_mul(t1, a1, a1),
          reads=[("amp",)], writes=[("t1",)])
    p.add("vector", lambda e: e.tensor_add(t0, t0, t1),
          reads=[("t0",), ("t1",)], writes=[("t0",)])
    ssum = scr2[:, 17:18]
    rsum = scr2[:, 18:19]
    p.add("vector", lambda e: e.reduce_sum(ssum, t0, axis=AX.X),
          reads=[("t0",)], writes=[("ssum",)])
    p.add("vector", lambda e: e.tensor_scalar_add(ssum, ssum, cfg.QEPS),
          reads=[("ssum",)], writes=[("ssum",)])
    p.add("vector", lambda e: e.reciprocal(rsum, ssum),
          reads=[("ssum",)], writes=[("rsum",)])
    p.add("vector", lambda e: e.tensor_scalar(t0, t0, rsum, None, ALU.mult),
          reads=[("t0",), ("rsum",)], writes=[("t0",)])
    p.add("vector", lambda e: e.tensor_scalar(amat[:, :], t0, esc_sb, None,
                                              ALU.mult),
          reads=[("t0",), ("esc",)], writes=[("amat",)])

    def layer_norm(xk, xa, outk, out_ap):
        nst = H // 512
        for i in range(nst):
            p.add("vector",
                  lambda e, i=i: e.bn_stats(ln6[:, 6 * i:6 * i + 6],
                                            xa[:, 512 * i:512 * i + 512]),
                  reads=[xk], writes=[("ln6", i)])
        mv = scr2[:, 20:22]
        p.add("vector",
              lambda e: e.bn_aggr(mv, ln6[:, 0:6 * nst].rearrange(
                  "p (n s) -> p n s", s=6)),
              reads=[("ln6", i) for i in range(nst)], writes=[("mv",)])
        ve = scr2[:, 22:23]
        p.add("vector", lambda e: e.tensor_scalar_add(ve, mv[:, 1:2], cfg.LNEPS),
              reads=[("mv",)], writes=[("ve",)])
        p.add("scalar", lambda e: e.activation(ve, ve, AF.Ln),
              reads=[("ve",)], writes=[("ve",)])
        rstd = scr2[:, 23:24]
        p.add("scalar", lambda e: e.activation(rstd, ve, AF.Exp, scale=-0.5),
              reads=[("ve",)], writes=[("rstd",)])
        nmr = scr2[:, 19:20]
        p.add("vector",
              lambda e: e.scalar_tensor_tensor(nmr, mv[:, 0:1], -1.0, rstd,
                                               ALU.mult, ALU.mult),
              reads=[("mv",), ("rstd",)], writes=[("nmr",)])
        p.add("scalar",
              lambda e: e.activation(out_ap, xa, AF.Identity, bias=nmr,
                                     scale=rstd),
              reads=[xk, ("nmr",), ("rstd",)], writes=[outk])

    def transpose128(dst_ap, dst_key, src_ap, src_key, chan):
        p.add("sync",
              lambda e: e.dma_start(dst_ap, src_ap, transpose=True),
              reads=[src_key], writes=[dst_key], dma_chan=chan, label="tr")

    def rope(ps_ap, ps_key, col0, n, out_ap, out_keys):
        s1, s2, s3 = scr[:, 0, 0:n], scr[:, 1, 0:n], scr[:, 2, 0:n]
        p.add("vector",
              lambda e: e.tensor_mul(s1, ps_ap, cos_sb[:, col0:col0 + n]),
              reads=[ps_key, ("cos",)], writes=[("rs1",)])
        p.add("vector", lambda e: e.stream_shuffle(s2, ps_ap, SHUF_MASK),
              reads=[ps_key], writes=[("rs2",)])
        p.add("vector",
              lambda e: e.tensor_mul(s3, s2, sin_sb[:, col0:col0 + n]),
              reads=[("rs2",), ("sin",)], writes=[("rs3",)])
        p.add("vector", lambda e: e.tensor_add(out_ap, s1, s3),
              reads=[("rs1",), ("rs3",)], writes=list(out_keys))

    def mm(bank_ap, bank_key, lhsT, rhs, reads, start, stop, tp=None,
           skip=False):
        extra = [] if start and not skip else [bank_key]
        return p.add(
            "tensor",
            lambda e: e.matmul(bank_ap, lhsT, rhs, start=start, stop=stop,
                               tile_position=tp, skip_group_check=skip),
            reads=list(reads) + extra, writes=[bank_key])

    def kv256(tab0, kcol0, st0, cb):
        kbanks = [(pbig, ("pbig",)), (pbig2, ("pbig2",)),
                  (P256[2], ("p256", 2)), (P256[3], ("p256", 3))]
        for oc in range(KC):
            bank, bkey = kbanks[oc % 4]
            for hc in range(HC):
                mm(bank[:, 0:256], bkey,
                   wk_sb[:, hc, 128 * oc:128 * oc + 128],
                   actT[:, hc, cb:cb + 256],
                   list(wk_pg) + [("actT", hc, cb), ("actT", hc, cb + 128)],
                   start=(hc == 0), stop=(hc == HC - 1))
            rope(bank[:, 0:256], bkey, tab0, 256,
                 kT[:, oc, kcol0:kcol0 + 256],
                 [("kT", oc, kcol0), ("kT", oc, kcol0 + 128)])
        for half in range(2):
            bank = pbig if half == 0 else pbig2
            bkey = ("pbig",) if half == 0 else ("pbig2",)
            for hc in range(HC):
                mm(bank[:, 0:KVD], bkey,
                   actT[:, hc, cb + 128 * half:cb + 128 * half + 128],
                   wv_sb[:, hc, :],
                   list(wv_pg) + [("actT", hc, cb + 128 * half)],
                   start=(hc == 0), stop=(hc == HC - 1))
            p.add("scalar",
                  lambda e, bank=bank, st=st0 + half: e.copy(
                      v_sb[:, st, :], bank[:, 0:KVD]),
                  reads=[bkey], writes=[("v", st0 + half)])

    def ln_and_transpose(xsrc, row0, dst_cb, chan_i):
        pp = (row0 // 128) % 2
        xa = x_buf[:, pp, :]
        xk = ("xbuf", pp)
        d("gpsimd", xa, xsrc[row0:row0 + 128, :], "D:x%d" % pp, writes=[xk])
        layer_norm(xk, xa, ("xn",), xn_bf[:, :])
        for hc in range(HC):
            bank = P256[hc % 2]
            bview = bank.bitcast(BF16)
            bkey = ("p256", hc % 2)
            p.add("tensor",
                  lambda e, hc=hc, bview=bview: e.transpose(
                      bview[:, 0:128], xn_bf[:, 128 * hc:128 * hc + 128],
                      id_bf[:, :]),
                  reads=[("xn",), ("id",)], writes=[bkey])
            p.add("vector",
                  lambda e, hc=hc, dst_cb=dst_cb, bview=bview: e.tensor_copy(
                      actT[:, hc, dst_cb:dst_cb + 128], bview[:, 0:128]),
                  reads=[bkey], writes=[("actT", hc, dst_cb)])

    d("gpsimd", wk_sb, wk_in[0], "D:wk", writes=list(wk_pg))
    d("gpsimd", wv_sb, wv_in[0], "D:wv", writes=list(wv_pg))
    for ot in range(NSLOT):
        cb = 256 * (ot % 2)
        for half in range(2):
            ln_and_transpose(x_oth, 256 * ot + 128 * half, cb + 128 * half,
                             2 * ot + half)
        kv256(OWN + 256 * ot, OWN + 256 * ot, OWN // 128 + 2 * ot, cb)

    NQ = int(os.environ.get("KK_NQ", cfg.NQ))
    KK_PH = int(os.environ.get("KK_PH", 9))
    for q in range(NQ):
        if q > 0:
            d("gpsimd", wk_sb, wk_in[0], "D:wk", writes=list(wk_pg))
            d("gpsimd", wv_sb, wv_in[0], "D:wv", writes=list(wv_pg))
        for g2 in range(2):
            for half in range(2):
                ln_and_transpose(x_own, 512 * q + 256 * g2 + 128 * half,
                                 256 * g2 + 128 * half, 2 * g2 + half)
            kv256(512 * q + 256 * g2, 512 * q + 256 * g2,
                  4 * q + 2 * g2, 256 * g2)

        if KK_PH < 2:
            continue
        d("gpsimd", wr_sb, wr_in[0], "D:wr", writes=list(wr_pg))
        for tt in range(4):
            bank = P256[tt % 2]
            bkey = ("p256", tt % 2)
            for hc in range(HC):
                mm(bank[:, 0:E], bkey, actT[:, hc, 128 * tt:128 * tt + 128],
                   wr_sb[:, hc, :],
                   list(wr_pg) + [("actT", hc, 128 * tt)],
                   start=(hc == 0), stop=(hc == HC - 1))
            nm = scr2[:, 48:49]
            p.add("vector",
                  lambda e, bank=bank: e.reduce_max(nm, bank[:, 0:E],
                                                    axis=AX.X, negate=True),
                  reads=[bkey], writes=[("nm",)])
            el = scr[:, 0, 0:E]
            Z = scr2[:, 49:50]
            p.add("scalar",
                  lambda e, bank=bank: e.activation(el, bank[:, 0:E], AF.Exp,
                                                    bias=nm, accum_out=Z),
                  reads=[bkey, ("nm",)], writes=[("rs1",), ("Z",)])
            mx = scr2[:, 53:61]
            p.add("vector", lambda e: e.max(out=mx, in_=el),
                  reads=[("rs1",)], writes=[("mx8",)])
            p.add("vector", lambda e: e.memset(scr2[:, 57:61], 0.0),
                  reads=[("mx8",)], writes=[("mx8",)])
            zap = scr[:, 1, 0:E]
            p.add("vector",
                  lambda e: e.match_replace(out=zap, in_to_replace=mx,
                                            in_values=el, imm_value=0.0),
                  reads=[("rs1",), ("mx8",)], writes=[("rs2",)])
            S = scr2[:, 50:51]
            p.add("vector",
                  lambda e: e.scalar_tensor_tensor(g_bf[:, :], zap, -1.0, el,
                                                   ALU.mult, ALU.add,
                                                   accum_out=S),
                  reads=[("rs1",), ("rs2",)], writes=[("gbf",), ("S",)])
            den = scr2[:, 51:52]
            p.add("vector",
                  lambda e: e.scalar_tensor_tensor(den, Z, cfg.QEPS, S,
                                                   ALU.mult, ALU.add),
                  reads=[("Z",), ("S",)], writes=[("den",)])
            rec = scr2[:, 52:53]
            p.add("vector", lambda e: e.reciprocal(rec, den),
                  reads=[("den",)], writes=[("rec",)])
            transpose128(gT_bf[:, :], ("gT",), g_bf[:, :], ("gbf",),
                         "D:tr%d" % (tt % 2))
            cbank = P256[2]
            mm(cbank[:, 0:KE], ("p256", 2), gT_bf[:, :], amat[:, :],
               [("gT",), ("amat",)], start=True, stop=True)
            p.add("vector",
                  lambda e, tt=tt, cbank=cbank: e.tensor_scalar(
                      c_sb[:, tt, :], cbank[:, 0:KE], rec, None, ALU.mult),
                  reads=[("p256", 2), ("rec",)], writes=[("c", tt)])

        if KK_PH < 3:
            continue
        nsub = (IE // 2) // CS
        steps = [(ih, sub, ke) for ih in range(2) for sub in range(nsub)
                 for ke in range(KE)]

        def gu_slab_idx(ih, sub, ke):
            return (ke * IE + ih * (IE // 2) + sub * CS) // CS

        def load_gu(si, buf):
            sl = gu_slab_idx(*steps[si])
            g_ap, g_pg = wgu[0][buf]
            u_ap, u_pg = wgu[1][buf]
            d("gpsimd", g_ap, wg_in[sl], "D:wg%d" % buf, writes=list(g_pg))
            d("gpsimd", u_ap, wu_in[sl], "D:wu%d" % buf, writes=list(u_pg))

        load_gu(0, 0)
        for si, (ih, sub, ke) in enumerate(steps):
            buf = si % 2
            if si + 1 < len(steps):
                load_gu(si + 1, (si + 1) % 2)
            g_ap, gpg = wgu[0][buf]
            u_ap, upg = wgu[1][buf]
            for tt in range(4):
                pg = P256[(si * 4 + tt) % 2]
                pgk = ("p256", (si * 4 + tt) % 2)
                pu = P256[2 + (si * 4 + tt) % 2]
                puk = ("p256", 2 + (si * 4 + tt) % 2)
                for hc in range(HC):
                    mm(pg[:, :], pgk, actT[:, hc, 128 * tt:128 * tt + 128],
                       g_ap[:, hc, :],
                       list(gpg) + [("actT", hc, 128 * tt)],
                       start=(hc == 0), stop=(hc == HC - 1))
                for hc in range(HC):
                    mm(pu[:, :], puk, actT[:, hc, 128 * tt:128 * tt + 128],
                       u_ap[:, hc, :],
                       list(upg) + [("actT", hc, 128 * tt)],
                       start=(hc == 0), stop=(hc == HC - 1))
                pp2 = tt % 2
                p.add("scalar",
                      lambda e, pg=pg, pp2=pp2: e.activation(
                          sg_bf[:, pp2, :], pg[:, :], AF.Silu),
                      reads=[pgk], writes=[("sg", pp2)])
                p.add("vector",
                      lambda e, pu=pu, pp2=pp2: e.tensor_copy(
                          su_bf[:, pp2, :], pu[:, :]),
                      reads=[puk], writes=[("su", pp2)])
                p.add("vector",
                      lambda e, pp2=pp2: e.tensor_mul(
                          bas_bf[:, pp2, :], sg_bf[:, pp2, :],
                          su_bf[:, pp2, :]),
                      reads=[("sg", pp2), ("su", pp2)], writes=[("bas", pp2)])
                yk = ("ya", tt)
                if ke == 0:
                    p.add("vector",
                          lambda e, pp2=pp2, tt=tt, ke=ke: e.tensor_scalar(
                              y_acc[:, tt, :], bas_bf[:, pp2, :],
                              c_sb[:, tt, ke:ke + 1], None, ALU.mult),
                          reads=[("bas", pp2), ("c", tt)], writes=[yk])
                elif ke < KE - 1:
                    p.add("vector",
                          lambda e, pp2=pp2, tt=tt, ke=ke:
                          e.scalar_tensor_tensor(
                              y_acc[:, tt, :], bas_bf[:, pp2, :],
                              c_sb[:, tt, ke:ke + 1],
                              y_acc[:, tt, :], ALU.mult, ALU.add),
                          reads=[("bas", pp2), ("c", tt), yk], writes=[yk])
                else:
                    p.add("vector",
                          lambda e, pp2=pp2, tt=tt, ke=ke:
                          e.scalar_tensor_tensor(
                              ybf_s[:, tt, :], bas_bf[:, pp2, :],
                              c_sb[:, tt, ke:ke + 1],
                              y_acc[:, tt, :], ALU.mult, ALU.add),
                          reads=[("bas", pp2), ("c", tt), yk],
                          writes=[("ybs", tt)])
            if ke == KE - 1:
                for tt in range(4):
                    for ib in range(CS // 128):
                        icol = ih * (IE // 2) + sub * CS + 128 * ib
                        ic = icol // 128
                        transpose128(
                            yT[:, ic, 128 * tt:128 * tt + 128],
                            ("yT", ic, 128 * tt),
                            ybf_s[:, tt, 128 * ib:128 * ib + 128],
                            ("ybs", tt), "D:tr%d" % (tt % 2))

        if KK_PH < 4:
            continue
        nsh = cfg.SHC // CS

        def load_sh(si, buf):
            g_ap, g_pg = wgu[0][buf]
            u_ap, u_pg = wgu[1][buf]
            d("gpsimd", g_ap, swg_in[si], "D:wg%d" % buf, writes=list(g_pg))
            d("gpsimd", u_ap, swu_in[si], "D:wu%d" % buf, writes=list(u_pg))

        load_sh(0, 0)
        for si in range(nsh):
            buf = si % 2
            if si + 1 < nsh:
                load_sh(si + 1, (si + 1) % 2)
            g_ap, gpg = wgu[0][buf]
            u_ap, upg = wgu[1][buf]
            for tt in range(4):
                pg = P256[(si * 4 + tt) % 2]
                pgk = ("p256", (si * 4 + tt) % 2)
                pu = P256[2 + (si * 4 + tt) % 2]
                puk = ("p256", 2 + (si * 4 + tt) % 2)
                for hc in range(HC):
                    mm(pg[:, :], pgk, actT[:, hc, 128 * tt:128 * tt + 128],
                       g_ap[:, hc, :],
                       list(gpg) + [("actT", hc, 128 * tt)],
                       start=(hc == 0), stop=(hc == HC - 1))
                for hc in range(HC):
                    mm(pu[:, :], puk, actT[:, hc, 128 * tt:128 * tt + 128],
                       u_ap[:, hc, :],
                       list(upg) + [("actT", hc, 128 * tt)],
                       start=(hc == 0), stop=(hc == HC - 1))
                pp2 = tt % 2
                p.add("scalar",
                      lambda e, pg=pg, pp2=pp2: e.activation(
                          sg_bf[:, pp2, :], pg[:, :], AF.Silu),
                      reads=[pgk], writes=[("sg", pp2)])
                p.add("vector",
                      lambda e, pu=pu, pp2=pp2: e.tensor_copy(
                          su_bf[:, pp2, :], pu[:, :]),
                      reads=[puk], writes=[("su", pp2)])
                p.add("vector",
                      lambda e, pp2=pp2: e.tensor_mul(
                          sh_bf[:, pp2, :], sg_bf[:, pp2, :], su_bf[:, pp2, :]),
                      reads=[("sg", pp2), ("su", pp2)], writes=[("bas", pp2)])
                for ib in range(CS // 128):
                    ic = (si * CS + 128 * ib) // 128
                    transpose128(shT[:, ic, 128 * tt:128 * tt + 128],
                                 ("shT", ic, 128 * tt),
                                 sh_bf[:, pp2, 128 * ib:128 * ib + 128],
                                 ("bas", pp2), "D:tr%d" % (tt % 2))

        if KK_PH < 5:
            continue
        def load_wq(si, buf):
            q_ap, q_pg = wq_b[buf]
            d("gpsimd", q_ap, wq_in[si], "D:wq%d" % buf, writes=list(q_pg))

        load_wq(0, 0)
        for si in range(HC // 2):
            buf = si % 2
            if si + 1 < HC // 2:
                load_wq(si + 1, (si + 1) % 2)
            q_ap, q_pg = wq_b[buf]
            for oc2 in range(2):
                oc = 2 * si + oc2
                bank = pbig if oc % 2 == 0 else pbig2
                bkey = ("pbig",) if oc % 2 == 0 else ("pbig2",)
                for hc in range(HC):
                    mm(bank[:, 0:512], bkey,
                       q_ap[:, hc, 128 * oc2:128 * oc2 + 128],
                       actT[:, hc, :],
                       list(q_pg) + [("actT", hc, o)
                                     for o in (0, 128, 256, 384)],
                       start=(hc == 0), stop=(hc == HC - 1))
                for hq in range(2):
                    rope(bank[:, 256 * hq:256 * hq + 256], bkey,
                         512 * q + 256 * hq, 256,
                         qT_v[:, oc, 256 * hq:256 * hq + 256],
                         [("qT", oc, 256 * hq)] + list(qT_pg))

        if KK_PH < 6:
            continue
        KK_NC = int(os.environ.get("KK_NC", 99))
        KK_TR = int(os.environ.get("KK_TR", 99))
        KK_NORM = int(os.environ.get("KK_NORM", 1))
        KK_EXP = int(os.environ.get("KK_EXP", 1))
        KK_PV = int(os.environ.get("KK_PV", 1))
        KK_SUM = int(os.environ.get("KK_SUM", 1))
        for sl in range(2):
            j = 2 * q + sl
            qc0 = 256 * sl
            ntr = min(cfg.trips(j), KK_TR)
            def emit_zq(c, hf):
                for ii in range(2):
                    qch = 4 * c + 2 * hf + ii
                    p.add("vector",
                          lambda e, qch=qch, ii=ii, qc0=qc0: e.tensor_copy(
                              zq[0:64, 2 * ii, :],
                              qT_v[0:64, qch, qc0:qc0 + 256]),
                          reads=[("qT", qch, qc0), qT_pg[0]],
                          writes=[("zq", 2 * ii)])
                    p.add("vector",
                          lambda e, qch=qch, ii=ii, qc0=qc0: e.tensor_copy(
                              zq[64:128, 2 * ii + 1, :],
                              qT_v[64:128, qch, qc0:qc0 + 256]),
                          reads=[("qT", qch, qc0), qT_pg[0]],
                          writes=[("zq", 2 * ii + 1)])

            def emit_scores(j, sc, diag_ok=True):
                if sc < 2 * j + 2:
                    kcol = 128 * sc
                else:
                    kcol = OWN + 128 * (sc - (2 * j + 2))
                diag = (sc == 2 * j) or (sc == 2 * j + 1)
                sreg = pbig if sc % 2 == 0 else pbig2
                srk = ("pbig",) if sc % 2 == 0 else ("pbig2",)
                for ii in range(2):
                    mm(sreg[:, 512 * ii:512 * ii + 256], srk,
                       kT[:, c, kcol:kcol + 128], zq[:, 2 * ii, :],
                       [("kT", c, kcol), ("zq", 2 * ii)],
                       start=True, stop=False)
                    mm(sreg[:, 512 * ii + 256:512 * ii + 512], srk,
                       kT[:, c, kcol:kcol + 128], zq[:, 2 * ii + 1, :],
                       [("kT", c, kcol), ("zq", 2 * ii + 1)],
                       start=False, stop=True)
                pp3 = sc % 3
                bcol = bias_sb[:, cfg.bias_col(j, sc):cfg.bias_col(j, sc) + 1]
                if not KK_EXP:
                    return
                if not diag:
                    p.add("scalar",
                          lambda e, sreg=sreg, pp3=pp3, bcol=bcol:
                          e.activation(exp_bf[:, pp3, :], sreg[:, :],
                                       AF.Exp, bias=bcol, scale=0.125),
                          reads=[srk, ("bias",)], writes=[("exp", pp3)])
                else:
                    dd = sc - 2 * j
                    if dd == 0:
                        exd, exdk = expd_bf, [("rs1",), ("rs2",)]
                    else:
                        exd = y_acc[:, 0:2, :].rearrange(
                            "p a b -> p (a b)").bitcast(BF16)
                        exdk = [("ya", 0), ("ya", 1)]
                    p.add("scalar",
                          lambda e, sreg=sreg, bcol=bcol, exd=exd:
                          e.activation(exd, sreg[:, :],
                                       AF.Exp, bias=bcol, scale=0.125),
                          reads=[srk, ("bias",)], writes=exdk)
                    p.add("vector",
                          lambda e, pp3=pp3, dd=dd, exd=exd: e.tensor_mul(
                              exp_bf[:, pp3, :].rearrange(
                                  "p (a b) -> p a b", a=4),
                              exd.rearrange("p (a b) -> p a b", a=4),
                              tri_sb[:, dd:dd + 1, :].to_broadcast(
                                  [128, 4, 256])),
                          reads=exdk + [("tri",)],
                          writes=[("exp", pp3)])

            def emit_pv(sc, ntr, st):
                if not (KK_EXP and KK_PV):
                    return
                pp3 = sc % 3
                ek = ("exp", pp3)
                first = (sc == 0)
                last = (sc == ntr - 1)
                for ii in range(2):
                    ex_lo = exp_bf[:, pp3, 512 * ii:512 * ii + 256]
                    ex_hi = exp_bf[:, pp3, 512 * ii + 256:512 * ii + 512]
                    mm(P256[ii][0:64, :], ("p256", ii),
                       v_sb[:, st, 64 * 2 * c:64 * 2 * c + 64],
                       ex_lo, [("v", st), ek],
                       start=first, stop=last, tp=(0, 0))
                    mm(P256[ii][64:128, :], ("p256", ii),
                       v_sb[:, st, 64 * (2 * c + 1):64 * (2 * c + 1) + 64],
                       ex_hi, [("v", st), ek],
                       start=first, stop=last, tp=(0, 64), skip=True)
                if KK_SUM:
                    for ii in range(2):
                        ex_lo = exp_bf[:, pp3, 512 * ii:512 * ii + 256]
                        ex_hi = exp_bf[:, pp3, 512 * ii + 256:512 * ii + 512]
                        mm(P256[2 + ii][0:64, :], ("p256", 2 + ii),
                           ones_bf[:, :], ex_lo, [("ones",), ek],
                           start=first, stop=last, tp=(0, 0))
                        mm(P256[2 + ii][64:128, :], ("p256", 2 + ii),
                           ones_bf[:, :], ex_hi, [("ones",), ek],
                           start=first, stop=last, tp=(0, 64), skip=True)

            passes = [(c, hf) for c in range(min(KC, KK_NC)) for hf in range(2)]
            emit_zq(*passes[0])
            for pi, (c, hf) in enumerate(passes):
                pend = []
                for sc in range(ntr):
                    st = sc if sc < 2 * j + 2 else OWN // 128 + (sc - (2 * j + 2))
                    emit_scores(j, sc)
                    while len(pend) >= 2:
                        args = pend.pop(0)
                        emit_pv(*args)
                    pend.append((sc, ntr, st))
                for args in pend:
                    emit_pv(*args)
                pend = []
                if pi + 1 < len(passes):
                    emit_zq(*passes[pi + 1])
                for ii in range(2 if KK_NORM else 0):
                    i = 2 * hf + ii
                    qch = 4 * c + i
                    pv = P256[ii]
                    pvk = ("p256", ii)
                    sm = P256[2 + ii]
                    smk = ("p256", 2 + ii)
                    rcp = scr[:, 2, :]
                    lns = scr[:, 1, :]
                    p.add("scalar",
                          lambda e, sm=sm, lns=lns: e.activation(
                              lns, sm[:, :], AF.Ln),
                          reads=[smk], writes=[("rs2",)])
                    p.add("scalar",
                          lambda e, rcp=rcp, lns=lns: e.activation(
                              rcp, lns, AF.Exp, scale=-1.0),
                          reads=[("rs2",)], writes=[("rs3",)])
                    p.add("vector",
                          lambda e, pv=pv, qch=qch, qc0=qc0, rcp=rcp:
                          e.tensor_mul(
                              actT[0:64, qch, qc0:qc0 + 256],
                              pv[0:64, :], rcp[0:64, :]),
                          reads=[pvk, ("rs3",)],
                          writes=[("actT", qch, qc0),
                                  ("actT", qch, qc0 + 128)])
                    p.add("vector",
                          lambda e, pv=pv, qch=qch, qc0=qc0, rcp=rcp:
                          e.tensor_mul(
                              actT[64:128, qch, qc0:qc0 + 256],
                              pv[64:128, :], rcp[64:128, :]),
                          reads=[pvk, ("rs3",)],
                          writes=[("actT", qch, qc0),
                                  ("actT", qch, qc0 + 128)])

        def load_b6(osi, buf):
            (o_ap, o_pg), (d_ap, d_pg), (s_ap, s_pg) = b6b[buf]
            d("gpsimd", o_ap, wo_in[osi], "D:b6o%d" % buf, writes=list(o_pg))
            d("gpsimd", d_ap, wd_in[osi], "D:b6d%d" % buf, writes=list(d_pg))
            d("gpsimd", s_ap, swd_in[osi], "D:b6s%d" % buf, writes=list(s_pg))

        nos = H // OS
        load_b6(0, 0)
        for osi in range(nos):
            buf = osi % 2
            if osi + 1 < nos:
                load_b6(osi + 1, (osi + 1) % 2)
            (o_ap, o_pg), (d_ap, d_pg), (s_ap, s_pg) = b6b[buf]
            for tt in range(4):
                bank = P256[(osi * 4 + tt) % 4]
                bkey = ("p256", (osi * 4 + tt) % 4)
                nmm = HC + NIC + NSC
                k_ = 0
                for hc in range(HC):
                    mm(bank[:, :], bkey, actT[:, hc, 128 * tt:128 * tt + 128],
                       o_ap[:, hc, :],
                       list(o_pg) + [("actT", hc, 128 * tt)],
                       start=(k_ == 0), stop=False)
                    k_ += 1
                for ic in range(NIC):
                    mm(bank[:, :], bkey, yT[:, ic, 128 * tt:128 * tt + 128],
                       d_ap[:, ic, :],
                       list(d_pg) + [("yT", ic, 128 * tt)],
                       start=False, stop=False)
                    k_ += 1
                for ic in range(NSC):
                    mm(bank[:, :], bkey, shT[:, ic, 128 * tt:128 * tt + 128],
                       s_ap[:, ic, :],
                       list(s_pg) + [("shT", ic, 128 * tt)],
                       start=False, stop=(ic == NSC - 1))
                    k_ += 1
                ppx = tt % 2
                xsl = xslice[:, ppx, :]
                xslk = ("xs", ppx)
                row0 = 512 * q + 128 * tt
                d("sync", xsl, x_own[row0:row0 + 128, OS * osi:OS * osi + OS],
                  "D:xs%d" % ppx, writes=[xslk])
                o_st = ost[:, ppx, :]
                ok = ("ost", ppx)
                p.add("vector",
                      lambda e, bank=bank, xsl=xsl, o_st=o_st: e.tensor_add(
                          o_st, bank[:, :], xsl),
                      reads=[bkey, xslk], writes=[ok])
                d("sync", out_d[row0:row0 + 128, OS * osi:OS * osi + OS],
                  o_st, "D:out", reads=[ok])

    with nc.allow_low_precision(reason="bf16 matmul operand staging"):
        with nc.Block() as block:
            p.emit(block)
    for t in reversed(ctx):
        t.__exit__(None, None, None)
    return nc, p


def kernel(**inputs):
    cfg = Cfg(tiny=False)
    shared, per_core = host_prep(cfg, inputs)
    nc, _ = build_nc(cfg)
    in_maps = []
    for core in range(cfg.NCORE):
        m = {"x_own": per_core[core]["x_own"],
             "x_oth": per_core[core]["x_oth"],
             "rcos": per_core[core]["rcos"],
             "rsin": per_core[core]["rsin"],
             "biasm": per_core[core]["biasm"],
             "tri": shared["tri"], "amps": shared["amps"],
             "escale": shared["escale"], "ident": shared["ident"],
             "wq": shared["wq"],
             "wk": shared["wk"], "wv": shared["wv"], "wo": shared["wo"],
             "wr": shared["wr"], "wg": shared["wg"], "wu": shared["wu"],
             "wd": shared["wd"], "swg": shared["swg"], "swu": shared["swu"],
             "swd": shared["swd"]}
        in_maps.append(m)
    res = run_bass_kernel_spmd(nc, in_maps, core_ids=list(range(cfg.NCORE)))
    out = np.zeros((cfg.B, cfg.T, cfg.H), np.float32)
    for core in range(cfg.NCORE):
        seq, par = core // 2, core % 2
        o = res.results[core]["out"].reshape(cfg.NSLOT, cfg.SLOT, cfg.H)
        for j in range(cfg.NSLOT):
            t = 2 * j + par
            out[seq, cfg.SLOT * t:cfg.SLOT * (t + 1), :] = o[j]
    return out
